# version 02
# baseline (speedup 1.0000x reference)
import contextlib
import numpy as np
import ml_dtypes
import concourse.bass as bass
import concourse.mybir as mybir
from concourse.bass_utils import run_bass_kernel_spmd

F32 = mybir.dt.float32
BF16 = mybir.dt.bfloat16
AF = mybir.ActivationFunctionType
ALU = mybir.AluOpType
BF = ml_dtypes.bfloat16
AXX = mybir.AxisListType.X

B, T, DM, H, DH = 4, 8192, 1024, 16, 64
C = 128
NCH = T // C
SEG1 = 1024
EPS = 1e-5
NT = T // 128


class Ctr:
    def __init__(self, h):
        self.h = h
        self.n = 0
    def inc(self, inst, k=1):
        inst.then_inc(self.h, k)
        self.n += k
        return self.n


def build_phase1(nc, debug=False, stages=255, ext=None, pfx=""):
    NTl = T // 128
    if ext is None:
        ext = {}
        for nm, shape, dt in [("x", [T, DM], F32), ("wqT", [DM, 512], F32),
                              ("wkT", [DM, 512], F32), ("waT", [DM, 512], F32),
                              ("waRT", [DM, 512], F32), ("wvbgT", [DM, 528], F32),
                              ("bav", [512], F32), ("baRv", [512], F32),
                              ("bbgv", [16], F32), ("lnwv", [DM], F32),
                              ("lnbv", [DM], F32), ("id16", [128, 128], BF16)]:
            ext[nm] = nc.dram_tensor(nm, shape, dt, kind="ExternalInput").ap()
        for nm, shape, dt in [("qd", [4, 128, T], BF16), ("kd", [4, 128, T], BF16),
                              ("lad", [4, 128, T], F32), ("laRd", [4, 128, T], F32),
                              ("vtd", [NTl, 128, 512], BF16), ("tgd", [NTl, 128, 26], F32),
                              ("abx", [NTl, 128, 2], F32)]:
            ext[nm] = nc.dram_tensor(nm, shape, dt, kind="ExternalOutput").ap()
    x, wqT, wkT, waT, waRT, wvbgT = (ext[n] for n in
        ("x", "wqT", "wkT", "waT", "waRT", "wvbgT"))
    bav, baRv, bbgv, lnwv, lnbv, id16 = (ext[n] for n in
        ("bav", "baRv", "bbgv", "lnwv", "lnbv", "id16"))
    qd, kd, lad, laRd, vtd, tgd, abx = (ext[n] for n in
        ("qd", "kd", "lad", "laRd", "vtd", "tgd", "abx"))
    if debug:
        xnd = nc.dram_tensor("xnd", [8, 128, T], BF16, kind="ExternalOutput").ap()
        statd = nc.dram_tensor("statd", [T // SEG1, 128, 96], F32, kind="ExternalOutput").ap()
        statd1 = nc.dram_tensor("statd1", [T // SEG1, 128, 96], F32, kind="ExternalOutput").ap()
        statd2 = nc.dram_tensor("statd2", [T // SEG1, 128, 96], F32, kind="ExternalOutput").ap()
        xzd = nc.dram_tensor("xzd", [T // SEG1, 128, 1024], BF16, kind="ExternalOutput").ap()

    NSEG = T // SEG1
    TPS = SEG1 // 128

    with contextlib.ExitStack() as es:
        sb = lambda n, s, d: es.enter_context(nc.sbuf_tensor(pfx + n, s, d))
        ps = lambda n, s, d: es.enter_context(nc.psum_tensor(pfx + n, s, d))
        mksem = lambda n: Ctr(es.enter_context(nc.semaphore(pfx + n)))

        wq16 = sb("wq16", [128, 4096], BF16)
        wk16 = sb("wk16", [128, 4096], BF16)
        wa16 = sb("wa16", [128, 4096], BF16)
        waR16 = sb("waR16", [128, 4096], BF16)
        wvbg16 = sb("wvbg16", [128, 8 * 528], BF16)
        wld = sb("wld", [128, 4224], F32)
        bacol = sb("bacol", [128, 4], F32)
        baRcol = sb("baRcol", [128, 4], F32)
        bbgr = sb("bbgr", [1, 16], BF16)
        ones1 = sb("ones1", [1, 128], BF16)
        lnwc = sb("lnwc", [128, 8], F32)
        lnbc = sb("lnbc", [128, 8], F32)
        sid = sb("sid", [128, 128], BF16)
        ones2 = sb("ones2", [128, 2], BF16)

        xt = sb("xt", [128, SEG1 * 8], F32)
        xsq = sb("xsq", [128, 1024], F32)
        stat = sb("stat", [128, 96], F32)
        xz = sb("xz", [128, 1024], BF16)
        xn2f = sb("xn2f", [128, 8 * SEG1], BF16)
        qdo = sb("qdo", [128, 4 * SEG1], BF16)
        kdo = sb("kdo", [128, 4 * SEG1], BF16)
        lao = sb("lao", [128, 4 * SEG1], F32)
        laRo = sb("laRo", [128, 4 * SEG1], F32)
        vto = sb("vto", [128, TPS * 512], BF16)
        tgo = sb("tgo", [128, TPS * 26], F32)
        ksq = sb("ksq", [128, 512], BF16)
        epsc = sb("epsc", [128, 1], F32)
        n2r = sb("n2r", [32, 4 * SEG1], F32)
        trsc = sb("trsc", [128, 128], F32)
        rnr = sb("rnr", [32, 4 * SEG1], F32)

        pbf = ps("pbf", [128, 3072], F32)
        pb16 = ps("pb16", [128, 2048], BF16)

        sd = mksem("sd")
        st_ = mksem("st_")
        sv = mksem("sv")
        ss = mksem("ss")
        so = mksem("so")

        SUMX, SUMS, NEGMU, STC, S1C, A2C, B2C, SCR, TM1, TM2 = (i * 8 for i in range(10))
        bank = lambda i: pbf[:, 512 * i:512 * (i + 1)]

        wlist = [(wqT, wq16, 512), (wkT, wk16, 512), (waT, wa16, 512),
                 (waRT, waR16, 512), (wvbgT, wvbg16, 528)]
        for src, dst, ncol in wlist:
            with nc.Block() as blk:
                @blk.sync
                def _(s, src=src, ncol=ncol):
                    for kc in range(8):
                        sd.inc(s.dma_start(wld[:, kc * ncol:(kc + 1) * ncol],
                                           src[kc * 128:(kc + 1) * 128, :]), 16)
                    s.wait_ge(sd.h, sd.n)
            with nc.Block() as blk:
                @blk.vector
                def _(v, dst=dst, ncol=ncol):
                    v.tensor_copy(dst[:, 0:8 * ncol], wld[:, 0:8 * ncol])
        with nc.Block() as blk:
            @blk.sync
            def _(s):
                for oc in range(4):
                    sd.inc(s.dma_start(bacol[:, oc:oc + 1], bav[oc * 128:(oc + 1) * 128]), 16)
                    sd.inc(s.dma_start(baRcol[:, oc:oc + 1], baRv[oc * 128:(oc + 1) * 128]), 16)
                for dc in range(8):
                    sd.inc(s.dma_start(lnwc[:, dc:dc + 1], lnwv[dc * 128:(dc + 1) * 128]), 16)
                    sd.inc(s.dma_start(lnbc[:, dc:dc + 1], lnbv[dc * 128:(dc + 1) * 128]), 16)
                sd.inc(s.dma_start(sid[:, :], id16), 16)
                sd.inc(s.dma_start(wld[0:1, 0:16], bbgv), 16)
                s.wait_ge(sd.h, sd.n)
        with nc.Block() as blk:
            @blk.vector
            def _(v):
                v.tensor_copy(bbgr[:, :], wld[0:1, 0:16])
                v.memset(epsc[:, :], EPS)
                v.memset(ones1[:, :], 1.0)
                v.memset(ones2[:, :], 0.0)
                v.memset(ones2[0:64, 0:1], 1.0)
                v.memset(ones2[64:128, 1:2], 1.0)

        for seg in range(NSEG):
            t0 = seg * SEG1
            with nc.Block() as blk:
                @blk.sync
                def _(s):
                    for tt in range(TPS):
                        sd.inc(s.dma_start(xt[:, tt * 1024:(tt + 1) * 1024],
                                           x[t0 + tt * 128: t0 + (tt + 1) * 128, :]), 16)
                    s.wait_ge(sd.h, sd.n)

            with nc.Block() as blk:
                @blk.scalar
                def _(sc):
                    for tt in range(TPS):
                        sc.activation(xsq[:, :], xt[:, tt * 1024:(tt + 1) * 1024],
                                      AF.Square, accum_out=stat[:, SUMS + tt:SUMS + tt + 1])
                @blk.vector
                def _(v):
                    for tt in range(TPS):
                        v.reduce_sum(stat[:, SUMX + tt:SUMX + tt + 1],
                                     xt[:, tt * 1024:(tt + 1) * 1024], axis=AXX)

            A = lambda c: stat[:, c:c + 8]
            with nc.Block() as blk:
                @blk.vector
                def _(v):
                    def vw(inst):
                        inst.then_inc(sv.h, 1)
                        sv.n += 1
                        v.wait_ge(sv.h, sv.n)
                    v.tensor_scalar(A(NEGMU), A(SUMX), -1.0 / DM, None, ALU.mult)
                    vw(v.tensor_scalar(A(SCR), A(SUMS), 1.0 / DM, None, ALU.mult))
                    vw(v.tensor_tensor(A(S1C), A(NEGMU), A(NEGMU), ALU.mult))
                    vw(v.tensor_tensor(A(SCR), A(SCR), A(S1C), ALU.subtract))
                    vw(v.tensor_scalar(A(TM1), A(SCR), EPS, None, ALU.add))
                    vw(v.tensor_scalar(A(TM2), A(SCR), 1.0 + EPS, EPS * EPS,
                                       ALU.mult, ALU.add))
            if debug:
                with nc.Block() as blk:
                    @blk.sync
                    def _(s):
                        sd.inc(s.dma_start(statd1[seg, :, :], stat[:, :]), 16)
                        s.wait_ge(sd.h, sd.n)
            with nc.Block() as blk:
                @blk.scalar
                def _(sc):
                    sc.activation(A(S1C), A(TM1), AF.Ln).then_inc(ss.h, 1)
                    ss.n += 1
                    sc.wait_ge(ss.h, ss.n)
                    sc.activation(A(S1C), A(S1C), AF.Exp, scale=-0.5).then_inc(ss.h, 1)
                    ss.n += 1
                    sc.activation(A(STC), A(TM2), AF.Ln).then_inc(ss.h, 1)
                    ss.n += 1
                    sc.wait_ge(ss.h, ss.n)
                    sc.activation(A(STC), A(STC), AF.Exp, scale=-0.5).then_inc(ss.h, 1)
                    ss.n += 1
            if debug:
                with nc.Block() as blk:
                    @blk.sync
                    def _(s):
                        sd.inc(s.dma_start(statd2[seg, :, :], stat[:, :]), 16)
                        s.wait_ge(sd.h, sd.n)
            with nc.Block() as blk:
                @blk.vector
                def _(v):
                    def vw(inst):
                        inst.then_inc(sv.h, 1)
                        sv.n += 1
                        v.wait_ge(sv.h, sv.n)
                    for base, dcol in ((S1C, TM1), (STC, TM2)):
                        vw(v.tensor_tensor(A(SCR), A(base), A(base), ALU.mult))
                        vw(v.tensor_tensor(A(SCR), A(SCR), A(dcol), ALU.mult))
                        vw(v.tensor_scalar(A(SCR), A(SCR), -0.5, 1.5, ALU.mult, ALU.add))
                        vw(v.tensor_tensor(A(base), A(base), A(SCR), ALU.mult))
                    v.tensor_tensor(A(B2C), A(NEGMU), A(S1C), ALU.mult)
                    vw(v.tensor_scalar(A(A2C), A(S1C), 1.0, None, ALU.add))
                    for tt in range(TPS):
                        v.tensor_copy(tgo[:, tt * 26 + 24:tt * 26 + 25],
                                      stat[:, A2C + tt:A2C + tt + 1])
                        v.tensor_copy(tgo[:, tt * 26 + 25:tt * 26 + 26],
                                      stat[:, B2C + tt:B2C + tt + 1])

            v0, t0n = sv.n, st_.n
            with nc.Block() as blk:
                @blk.vector
                def _(v):
                    for tt in range(TPS):
                        if tt > 0:
                            v.wait_ge(st_.h, t0n + 8 * tt)
                        v.tensor_scalar(xz[:, :], xt[:, tt * 1024:(tt + 1) * 1024],
                                        stat[:, NEGMU + tt:NEGMU + tt + 1],
                                        stat[:, STC + tt:STC + tt + 1],
                                        ALU.add, ALU.mult).then_inc(sv.h, 1)
                        v.wait_ge(st_.h, t0n + 8 * (tt + 1))
                        for dc in range(8):
                            v.tensor_scalar(
                                xn2f[:, dc * SEG1 + tt * 128: dc * SEG1 + (tt + 1) * 128],
                                pb16[:, dc * 128:(dc + 1) * 128],
                                lnwc[:, dc:dc + 1], lnbc[:, dc:dc + 1],
                                ALU.mult, ALU.add).then_inc(sv.h, 1)
                @blk.tensor
                def _(t):
                    for tt in range(TPS):
                        t.wait_ge(sv.h, v0 + 9 * tt + 1)
                        for dc in range(8):
                            t.transpose(pb16[:, dc * 128:(dc + 1) * 128],
                                        xz[:, dc * 128:(dc + 1) * 128],
                                        sid[:, :]).then_inc(st_.h, 1)
            sv.n = v0 + 9 * TPS
            st_.n = t0n + 8 * TPS

            wsrc = [wq16, wk16, wa16, waR16]
            groups = [(ti, oc) for ti in range(4) for oc in range(4)]
            tb, vb, sb_ = st_.n, sv.n, ss.n
            if not (stages & 1):
                groups = []
            with nc.Block() as blk:
                @blk.tensor
                def _(t):
                    for mt in range(2):
                        for gi, (ti, oc) in enumerate(groups):
                            gidx = mt * 16 + gi
                            bk = gidx % 6
                            if gidx >= 6:
                                j = gidx - 6
                                jmt, jgi = divmod(j, 16)
                                jti = groups[jgi][0]
                                nvec = sum(1 for m2 in range(2) for g2, (t2, _) in enumerate(groups)
                                           if m2 * 16 + g2 <= j and t2 < 2)
                                nsc = (j + 1) - nvec
                                if jti < 2:
                                    t.wait_ge(sv.h, vb + nvec)
                                else:
                                    t.wait_ge(ss.h, sb_ + nsc)
                            mm = None
                            for kc in range(8):
                                mm = t.matmul(
                                    bank(bk)[:, 0:512],
                                    wsrc[ti][:, kc * 512 + oc * 128: kc * 512 + (oc + 1) * 128],
                                    xn2f[:, kc * SEG1 + mt * 512: kc * SEG1 + mt * 512 + 512],
                                    start=(kc == 0), stop=(kc == 7))
                            mm.then_inc(st_.h, 1)
                @blk.vector
                def _(v):
                    for mt in range(2):
                        for gi, (ti, oc) in enumerate(groups):
                            if ti >= 2:
                                continue
                            gidx = mt * 16 + gi
                            bk = gidx % 6
                            v.wait_ge(st_.h, tb + gidx + 1)
                            co = oc * SEG1 + mt * 512
                            dst = qdo if ti == 0 else kdo
                            v.tensor_copy(dst[:, co:co + 512], bank(bk)[:, 0:512]
                                          ).then_inc(sv.h, 1)
                @blk.scalar
                def _(sc):
                    for mt in range(2):
                        for gi, (ti, oc) in enumerate(groups):
                            if ti < 2:
                                continue
                            gidx = mt * 16 + gi
                            bk = gidx % 6
                            sc.wait_ge(st_.h, tb + gidx + 1)
                            co = oc * SEG1 + mt * 512
                            bc = bacol if ti == 2 else baRcol
                            dst = lao if ti == 2 else laRo
                            sc.activation(dst[:, co:co + 512], bank(bk)[:, 0:512],
                                          AF.Sigmoid, bias=bc[:, oc:oc + 1]
                                          ).then_inc(ss.h, 1)
            st_.n = tb + 2 * len(groups)
            sv.n = vb + len(groups)
            ss.n = sb_ + len(groups)

            tb, vb, sb_ = st_.n, sv.n, ss.n
            NRM = 8 if (stages & 2) else 0
            with nc.Block() as blk:
                @blk.scalar
                def _(sc):
                    for i in range(NRM):
                        oc, mt = divmod(i, 2)
                        if i > 0:
                            sc.wait_ge(st_.h, tb + i)
                        co = oc * SEG1 + mt * 512
                        sc.activation(ksq[:, :], kdo[:, co:co + 512], AF.Square
                                      ).then_inc(ss.h, 1)
                @blk.tensor
                def _(t):
                    for i in range(NRM):
                        t.wait_ge(ss.h, sb_ + i + 1)
                        if i > 0:
                            t.wait_ge(sv.h, vb + i)
                        t.matmul(pbf[0:2, 2560:3072], ones2[:, :], ksq[:, :],
                                 start=True, stop=True).then_inc(st_.h, 1)
                @blk.vector
                def _(v):
                    for i in range(NRM):
                        oc, mt = divmod(i, 2)
                        v.wait_ge(st_.h, tb + i + 1)
                        v.tensor_copy(n2r[0:2, oc * SEG1 + mt * 512: oc * SEG1 + mt * 512 + 512],
                                      pbf[0:2, 2560:3072]).then_inc(sv.h, 1)
            st_.n = tb + NRM
            sv.n = vb + NRM
            ss.n = sb_ + NRM

            with nc.Block() as blk:
                @blk.vector
                def _(v):
                    v.tensor_scalar(n2r[0:2, 0:SEG1 * 4], n2r[0:2, 0:SEG1 * 4],
                                    1e-24, None, ALU.max)
            with nc.Block() as blk:
                @blk.scalar
                def _(sc):
                    if not (stages & 8):
                        return
                    sc.activation(lao[:, :], lao[:, :], AF.Ln)
                    sc.activation(laRo[:, :], laRo[:, :], AF.Ln)
                    sc.activation(rnr[0:2, :], n2r[0:2, :], AF.Ln).then_inc(ss.h, 1)
                    ss.n += 1
                    sc.wait_ge(ss.h, ss.n)
                    sc.activation(rnr[0:2, :], rnr[0:2, :],
                                  AF.Exp, scale=-0.5).then_inc(ss.h, 1)
                    ss.n += 1
            with nc.Block() as blk:
                @blk.vector
                def _(v):
                    def vw(inst):
                        inst.then_inc(sv.h, 1)
                        sv.n += 1
                        v.wait_ge(sv.h, sv.n)
                    for ch in range(4):
                        rna = rnr[0:2, ch * SEG1:(ch + 1) * SEG1]
                        n2c = n2r[0:2, ch * SEG1:(ch + 1) * SEG1]
                        tmp = xsq[0:2, 0:SEG1]
                        vw(v.tensor_tensor(tmp, rna, rna, ALU.mult))
                        vw(v.tensor_tensor(tmp, tmp, n2c, ALU.mult))
                        vw(v.tensor_scalar(tmp, tmp, -0.5, 1.5, ALU.mult, ALU.add))
                        vw(v.tensor_tensor(rna, rna, tmp, ALU.mult))

            with nc.Block() as blk:
                @blk.vector
                def _(v):
                    for tt in range(TPS):
                        last = None
                        for oc in range(4):
                            for bi in range(4):
                                last = v.transpose(
                                    trsc[32 * bi:32 * (bi + 1), oc * 32:oc * 32 + 32],
                                    rnr[0:32, oc * SEG1 + tt * 128 + 32 * bi:
                                        oc * SEG1 + tt * 128 + 32 * (bi + 1)])
                        last.then_inc(sv.h, 1)
                        sv.n += 1
                        v.wait_ge(sv.h, sv.n)
                        v.tensor_copy(
                            tgo[:, tt * 26 + 16: tt * 26 + 24].rearrange("p (a b) -> p a b", a=4),
                            trsc[:, :].rearrange("p (a b) -> p a b", a=4)[:, :, 0:2])

            tb, vb, sb_ = st_.n, sv.n, ss.n
            with nc.Block() as blk:
                @blk.tensor
                def _(t):
                    for tt in range(TPS if (stages & 4) else 0):
                        bkv = tt % 2
                        bkg = 2 + tt % 2
                        if tt >= 2:
                            t.wait_ge(sv.h, vb + tt - 2 + 1)
                            t.wait_ge(ss.h, sb_ + tt - 2 + 1)
                        for kc in range(8):
                            t.matmul(bank(bkv)[:, 0:512],
                                     xn2f[:, kc * SEG1 + tt * 128: kc * SEG1 + (tt + 1) * 128],
                                     wvbg16[:, kc * 528: kc * 528 + 512],
                                     start=(kc == 0), stop=(kc == 7))
                            t.matmul(bank(bkg)[:, 0:16],
                                     xn2f[:, kc * SEG1 + tt * 128: kc * SEG1 + (tt + 1) * 128],
                                     wvbg16[:, kc * 528 + 512: kc * 528 + 528],
                                     start=(kc == 0), stop=False)
                        t.matmul(bank(bkg)[:, 0:16], ones1[:, :], bbgr[:, :],
                                 start=False, stop=True).then_inc(st_.h, 1)
                @blk.vector
                def _(v):
                    for tt in range(TPS if (stages & 4) else 0):
                        bkv = tt % 2
                        v.wait_ge(st_.h, tb + tt + 1)
                        v.tensor_copy(vto[:, tt * 512:(tt + 1) * 512],
                                      bank(bkv)[:, 0:512]).then_inc(sv.h, 1)
                @blk.scalar
                def _(sc):
                    for tt in range(TPS if (stages & 4) else 0):
                        bkg = 2 + tt % 2
                        sc.wait_ge(st_.h, tb + tt + 1)
                        sc.activation(tgo[:, tt * 26:tt * 26 + 16], bank(bkg)[:, 0:16],
                                      AF.Sigmoid).then_inc(ss.h, 1)
            if stages & 4:
                st_.n = tb + TPS
                sv.n = vb + TPS
                ss.n = sb_ + TPS

            with nc.Block() as blk:
                @blk.sync
                def _(s):
                    for oc in range(4):
                        so.inc(s.dma_start(qd[oc, :, t0:t0 + SEG1],
                                           qdo[:, oc * SEG1:(oc + 1) * SEG1]), 16)
                        so.inc(s.dma_start(kd[oc, :, t0:t0 + SEG1],
                                           kdo[:, oc * SEG1:(oc + 1) * SEG1]), 16)
                        so.inc(s.dma_start(lad[oc, :, t0:t0 + SEG1],
                                           lao[:, oc * SEG1:(oc + 1) * SEG1]), 16)
                        so.inc(s.dma_start(laRd[oc, :, t0:t0 + SEG1],
                                           laRo[:, oc * SEG1:(oc + 1) * SEG1]), 16)
                    if debug:
                        so.inc(s.dma_start(statd[seg, :, :], stat[:, :]), 16)
                        so.inc(s.dma_start(xzd[seg, :, :], xz[:, :]), 16)
                        for dc in range(8):
                            so.inc(s.dma_start(xnd[dc, :, t0:t0 + SEG1],
                                               xn2f[:, dc * SEG1:(dc + 1) * SEG1]), 16)
                    for tt in range(TPS):
                        so.inc(s.dma_start(vtd[seg * TPS + tt, :, :],
                                           vto[:, tt * 512:(tt + 1) * 512]), 16)
                        so.inc(s.dma_start(tgd[seg * TPS + tt, :, :],
                                           tgo[:, tt * 26:(tt + 1) * 26]), 16)
                        so.inc(s.dma_start(abx[seg * TPS + tt, :, :],
                                           tgo[:, tt * 26 + 24:(tt + 1) * 26]), 16)
                    s.wait_ge(so.h, so.n)
    return nc


def _hostprep_phase1(inputs, core):
    b, hh = core // 2, core % 2
    hs = slice(512 * hh, 512 * hh + 512)
    hs8 = slice(8 * hh, 8 * hh + 8)
    f = np.float32
    cc = np.ascontiguousarray
    wvbgT = np.concatenate([
        np.asarray(inputs["L0_attn_Wv"], f)[hs].T,
        np.asarray(inputs["L0_attn_Wb"], f)[hs8].T,
        np.asarray(inputs["L0_attn_Wg"], f)[hs8].T], axis=1)
    return {
        "x": cc(np.asarray(inputs["L0_attn_x"], f)[b]),
        "wqT": cc(np.asarray(inputs["L0_attn_Wq"], f)[hs].T),
        "wkT": cc(np.asarray(inputs["L0_attn_Wk"], f)[hs].T),
        "waT": cc(np.asarray(inputs["L0_attn_Wa"], f)[hs].T),
        "waRT": cc(np.asarray(inputs["L0_attn_WaR"], f)[hs].T),
        "wvbgT": cc(wvbgT),
        "bav": cc(np.asarray(inputs["L0_attn_ba"], f)[hs]),
        "baRv": cc(np.asarray(inputs["L0_attn_baR"], f)[hs]),
        "bbgv": cc(np.concatenate([np.asarray(inputs["L0_attn_bb"], f)[hs8],
                                   np.asarray(inputs["L0_attn_bg"], f)[hs8]])),
        "lnwv": cc(np.asarray(inputs["L0_attn_ln_w"], f)),
        "lnbv": cc(np.asarray(inputs["L0_attn_ln_b"], f)),
        "id16": np.eye(128, dtype=BF).view(np.uint16),
    }


def kernel(**inputs):
    raise NotImplementedError("phases under construction")


def build_phase2(nc, stages2=255, ext=None, pfx=""):
    if ext is None:
        ext = {}
        for nm, shape, dt in [("qd", [4, 128, T], BF16), ("kd", [4, 128, T], BF16),
                              ("lad", [4, 128, T], F32), ("laRd", [4, 128, T], F32),
                              ("vtd", [T // 128, 128, 512], BF16),
                              ("tgd", [T // 128, 128, 26], F32),
                              ("id16", [128, 128], BF16), ("mks", [128, 3 * 128], BF16)]:
            ext[nm] = nc.dram_tensor(nm, shape, dt, kind="ExternalInput").ap()
        ext["ofd"] = nc.dram_tensor("ofd", [8, 128, T // 2], BF16, kind="ExternalOutput").ap()
    qd, kd, lad, laRd, vtd, tgd, id16, mks, ofd = (ext[n] for n in
        ("qd", "kd", "lad", "laRd", "vtd", "tgd", "id16", "mks", "ofd"))

    NSEG = T // 512
    NCHl = T // C
    with contextlib.ExitStack() as es:
        sb = lambda n, s, d: es.enter_context(nc.sbuf_tensor(pfx + n, s, d))
        ps = lambda n, s, d: es.enter_context(nc.psum_tensor(pfx + n, s, d))
        mksem = lambda n: Ctr(es.enter_context(nc.semaphore(pfx + n)))

        sid = sb("sid", [128, 128], BF16)
        mLS = sb("mLS", [128, 128], BF16)
        mUS = sb("mUS", [128, 128], BF16)
        mUI = sb("mUI", [128, 128], BF16)
        i16 = sb("i16", [128, 128], BF16)

        qds = sb("qds", [128, 2 * 2048], BF16)
        kds = sb("kds", [128, 2 * 2048], BF16)
        las = sb("las", [128, 2 * 2048], F32)
        laRs = sb("laRs", [128, 2 * 2048], F32)
        vts = sb("vts", [128, 2 * 2048], BF16)
        tgs = sb("tgs", [128, 2 * 104], F32)
        ofo = sb("ofo", [128, 2 * 2048], BF16)

        labA = [sb(f"labA{p}", [128, 129], F32) for p in range(4)]
        labR = [sb(f"labR{p}", [128, 129], F32) for p in range(4)]
        ncpA = sb("ncpA", [128, 4], F32)
        pcpA = sb("pcpA", [128, 4], F32)
        ncpR = sb("ncpR", [128, 4], F32)
        pcpR = sb("pcpR", [128, 4], F32)
        eclA = sb("eclA", [128, 4], F32)
        eclR = sb("eclR", [128, 4], F32)
        EpA = [sb(f"EpA{p}", [128, 128], F32) for p in range(4)]
        EmA = [sb(f"EmA{p}", [128, 128], F32) for p in range(4)]
        EpP = [sb(f"EpP{p}", [128, 128], F32) for p in range(4)]
        EpR = [sb(f"EpR{p}", [128, 128], F32) for p in range(4)]
        EmR = [sb(f"EmR{p}", [128, 128], F32) for p in range(4)]
        kA = sb("kA", [128, 4 * 128], BF16)
        kIA = sb("kIA", [128, 4 * 128], BF16)
        kPRE = sb("kPRE", [128, 4 * 128], BF16)
        qA = sb("qA", [128, 4 * 128], BF16)
        kAR = sb("kAR", [128, 4 * 128], BF16)
        kIAR = sb("kIAR", [128, 4 * 128], BF16)
        qAR = sb("qAR", [128, 4 * 128], BF16)
        kiat = sb("kiat", [128, 4 * 128], BF16)
        kiart = sb("kiart", [128, 4 * 128], BF16)
        vsb = sb("vsb", [128, 512], BF16)
        scl = sb("scl", [128, 64], F32)
        BRN, NBRN2, GRN, NGRN2, NRN, SC5, SC6, SC7 = (i * 8 for i in range(8))

        nbt = sb("nbt", [128, 16 * 128], BF16)
        nbtT = sb("nbtT", [128, 16 * 128], BF16)
        pwA = sb("pwA", [128, 16 * 128], BF16)
        pwAT = sb("pwAT", [128, 16 * 128], BF16)
        pwB = sb("pwB", [128, 16 * 128], BF16)
        pwBT = sb("pwBT", [128, 16 * 128], BF16)
        Tc = sb("Tc", [128, 16 * 128], BF16)
        gq = sb("gq", [128, 8 * 128], BF16)
        gqR = sb("gqR", [128, 8 * 128], BF16)
        gpre = sb("gpre", [128, 8 * 128], BF16)
        xs = sb("xs", [128, 8 * 64], BF16)
        ub = sb("ub", [128, 8 * 64], BF16)
        ubR = sb("ubR", [128, 8 * 64], BF16)
        rs = sb("rs", [128, 8 * 64], BF16)
        rtmp = sb("rtmp", [128, 8 * 64], F32)
        oT = sb("oT", [128, 8 * 64], BF16)
        spr = sb("spr", [128, 4 * 128], BF16)

        pbf = ps("pbf", [128, 3072], F32)
        pb16 = ps("pb16", [128, 2048], BF16)
        bank = lambda i: pbf[:, 512 * i:512 * (i + 1)]

        sd = mksem("sd")
        st_ = mksem("st_")
        sv = mksem("sv")
        ss = mksem("ss")
        so = mksem("so")

        def seg_load(s, seg, buf):
            t0 = seg * 512
            for oc in range(4):
                sd.inc(s.dma_start(qds[:, buf * 2048 + oc * 512: buf * 2048 + (oc + 1) * 512],
                                   qd[oc, :, t0:t0 + 512]), 16)
                sd.inc(s.dma_start(kds[:, buf * 2048 + oc * 512: buf * 2048 + (oc + 1) * 512],
                                   kd[oc, :, t0:t0 + 512]), 16)
                sd.inc(s.dma_start(las[:, buf * 2048 + oc * 512: buf * 2048 + (oc + 1) * 512],
                                   lad[oc, :, t0:t0 + 512]), 16)
                sd.inc(s.dma_start(laRs[:, buf * 2048 + oc * 512: buf * 2048 + (oc + 1) * 512],
                                   laRd[oc, :, t0:t0 + 512]), 16)
            for tt in range(4):
                sd.inc(s.dma_start(vts[:, buf * 2048 + tt * 512: buf * 2048 + (tt + 1) * 512],
                                   vtd[seg * 4 + tt, :, :]), 16)
                sd.inc(s.dma_start(tgs[:, buf * 104 + tt * 26: buf * 104 + (tt + 1) * 26],
                                   tgd[seg * 4 + tt, :, :]), 16)
            return sd.n

        with nc.Block() as blk:
            @blk.sync
            def _(s):
                sd.inc(s.dma_start(sid[:, :], id16), 16)
                sd.inc(s.dma_start(mLS[:, :], mks[:, 0:128]), 16)
                sd.inc(s.dma_start(mUS[:, :], mks[:, 128:256]), 16)
                sd.inc(s.dma_start(mUI[:, :], mks[:, 256:384]), 16)
                seg_load(s, 0, 0)
                s.wait_ge(sd.h, sd.n)
            @blk.vector
            def _(v):
                v.memset(spr[:, :], 0.0)
                for p in range(4):
                    v.memset(labA[p][:, 0:1], 0.0)
                    v.memset(labR[p][:, 0:1], 0.0)
                v.tensor_copy(i16[:, :], i16[:, :])
        with nc.Block() as blk:
            @blk.vector
            def _(v):
                v.tensor_copy(i16[:, :], sid[:, :])

        segload_done = [sd.n]

        for ci in range(NCHl):
            seg, cis = divmod(ci, 4)
            buf = seg % 2
            tgc = lambda c0, n=8: tgs[:, buf * 104 + cis * 26 + c0: buf * 104 + cis * 26 + c0 + n]
            qsl = lambda p: qds[:, buf * 2048 + p * 512 + cis * 128: buf * 2048 + p * 512 + (cis + 1) * 128]
            ksl = lambda p: kds[:, buf * 2048 + p * 512 + cis * 128: buf * 2048 + p * 512 + (cis + 1) * 128]
            lasl = lambda p: las[:, buf * 2048 + p * 512 + cis * 128: buf * 2048 + p * 512 + (cis + 1) * 128]
            laRsl = lambda p: laRs[:, buf * 2048 + p * 512 + cis * 128: buf * 2048 + p * 512 + (cis + 1) * 128]
            vsl = lambda h: vts[:, buf * 2048 + cis * 512 + h * 64: buf * 2048 + cis * 512 + (h + 1) * 64]
            ofsl = ofo[:, (seg % 2) * 2048:]

            if cis == 0:
                if seg + 1 < NSEG:
                    with nc.Block() as blk:
                        @blk.scalar
                        def _(s):
                            s.wait_ge(so.h, so.n)
                            seg_load(s, seg + 1, (seg + 1) % 2)
                    segload_done.append(sd.n)
                with nc.Block() as blk:
                    @blk.sync
                    def _(s):
                        s.wait_ge(sd.h, segload_done[seg])

            tb, vb, sb0 = st_.n, sv.n, ss.n
            if not (stages2 & 1):
                continue
            with nc.Block() as blk:
                @blk.vector
                def _(v):
                    def vw(inst):
                        inst.then_inc(sv.h, 1); sv.n += 1; v.wait_ge(sv.h, sv.n)
                    v.tensor_tensor(scl[:, BRN:BRN + 8], tgc(0), tgc(16), ALU.mult)
                    v.tensor_tensor(scl[:, GRN:GRN + 8], tgc(8), tgc(16), ALU.mult)
                    vw(v.tensor_scalar(scl[:, NRN:NRN + 8], tgc(16), -1.0, None, ALU.mult))
                    v.tensor_tensor(scl[:, NBRN2:NBRN2 + 8], scl[:, BRN:BRN + 8], scl[:, NRN:NRN + 8], ALU.mult)
                    vw(v.tensor_tensor(scl[:, NGRN2:NGRN2 + 8], scl[:, GRN:GRN + 8], scl[:, NRN:NRN + 8], ALU.mult))
                    last = None
                    for p in range(4):
                        last = v.tensor_tensor_scan(labA[p][:, 1:129], lasl(p), lasl(p), 0.0,
                                                    ALU.add, ALU.bypass)
                        last = v.tensor_tensor_scan(labR[p][:, 1:129], laRsl(p), laRsl(p), 0.0,
                                                    ALU.add, ALU.bypass)
                    last.then_inc(sv.h, 1); sv.n += 1
                    v.wait_ge(sv.h, sv.n)
                    for p in range(4):
                        v.tensor_scalar(ncpA[:, p:p + 1], labA[p][:, 128:129], -0.5, None, ALU.mult)
                        v.tensor_scalar(pcpA[:, p:p + 1], labA[p][:, 128:129], 0.5, None, ALU.mult)
                        v.tensor_scalar(ncpR[:, p:p + 1], labR[p][:, 128:129], -0.5, None, ALU.mult)
                        last = v.tensor_scalar(pcpR[:, p:p + 1], labR[p][:, 128:129], 0.5, None, ALU.mult)
                    last.then_inc(sv.h, 1); sv.n += 1
                @blk.scalar
                def _(sc):
                    sc.wait_ge(sv.h, vb + 4)
                    last = None
                    for p in range(4):
                        sc.activation(EpA[p][:, :], labA[p][:, 1:129], AF.Exp, bias=ncpA[:, p:p + 1])
                        sc.activation(EmA[p][:, :], labA[p][:, 1:129], AF.Exp, scale=-1.0, bias=pcpA[:, p:p + 1])
                        sc.activation(EpP[p][:, :], labA[p][:, 0:128], AF.Exp, bias=ncpA[:, p:p + 1])
                        sc.activation(EpR[p][:, :], labR[p][:, 1:129], AF.Exp, bias=ncpR[:, p:p + 1])
                        sc.activation(EmR[p][:, :], labR[p][:, 1:129], AF.Exp, scale=-1.0, bias=pcpR[:, p:p + 1])
                        sc.activation(eclA[:, p:p + 1], labA[p][:, 128:129], AF.Exp, scale=0.5)
                        last = sc.activation(eclR[:, p:p + 1], labR[p][:, 128:129], AF.Exp, scale=0.5)
                    last.then_inc(ss.h, 1); ss.n += 1
            tb, vb, sb0 = st_.n, sv.n, ss.n
            with nc.Block() as blk:
                @blk.vector
                def _(v):
                    v.wait_ge(ss.h, sb0)
                    last = None
                    for p in range(4):
                        sl = slice(p * 128, (p + 1) * 128)
                        v.tensor_tensor(kA[:, sl], ksl(p), EpA[p][:, :], ALU.mult)
                        v.tensor_tensor(kIA[:, sl], ksl(p), EmA[p][:, :], ALU.mult)
                        v.tensor_tensor(kPRE[:, sl], ksl(p), EpP[p][:, :], ALU.mult)
                        v.tensor_tensor(qA[:, sl], qsl(p), EpA[p][:, :], ALU.mult)
                        v.tensor_tensor(kAR[:, sl], ksl(p), EpR[p][:, :], ALU.mult)
                        v.tensor_tensor(kIAR[:, sl], ksl(p), EmR[p][:, :], ALU.mult)
                        last = v.tensor_tensor(qAR[:, sl], qsl(p), EpR[p][:, :], ALU.mult)
                    for h in range(8):
                        last = v.tensor_scalar(vsb[:, h * 64:(h + 1) * 64], vsl(h),
                                               scl[:, BRN + h:BRN + h + 1], None, ALU.mult)
                    for p in range(4):
                        v.tensor_scalar(spr[:, p * 128:p * 128 + 64], spr[:, p * 128:p * 128 + 64],
                                        eclA[:, p:p + 1], None, ALU.mult)
                        last = v.tensor_scalar(spr[:, p * 128 + 64:p * 128 + 128],
                                               spr[:, p * 128 + 64:p * 128 + 128],
                                               eclR[:, p:p + 1], None, ALU.mult)
                    last.then_inc(sv.h, 1); sv.n += 1
                @blk.tensor
                def _(t):
                    t.wait_ge(sv.h, sv.n)
                    for p in range(4):
                        t.transpose(pb16[:, p * 128:(p + 1) * 128], kIA[:, p * 128:(p + 1) * 128],
                                    sid[:, :]).then_inc(st_.h, 1)
                        t.transpose(pb16[:, 512 + p * 128:512 + (p + 1) * 128],
                                    kIAR[:, p * 128:(p + 1) * 128], sid[:, :]).then_inc(st_.h, 1)
                    st_.n += 8
                @blk.scalar
                def _(sc):
                    sc.wait_ge(st_.h, tb + 8)
                    for p in range(4):
                        sc.copy(kiat[:, p * 128:(p + 1) * 128],
                                pb16[:, p * 128:(p + 1) * 128]).then_inc(ss.h, 1)
                        sc.copy(kiart[:, p * 128:(p + 1) * 128],
                                pb16[:, 512 + p * 128:512 + (p + 1) * 128]).then_inc(ss.h, 1)
                        ss.n += 2

            if not (stages2 & 2):
                continue
            units = [(h, s_) for h in range(8) for s_ in range(2)]
            tb, vb, sb0 = st_.n, sv.n, ss.n
            with nc.Block() as blk:
                gslot = {}
                evn = [ui for ui, (h, s_) in enumerate(units) if h % 2 == 0]
                odd = [ui for ui, (h, s_) in enumerate(units) if h % 2 == 1]
                for j, ui in enumerate(evn):
                    gslot[ui] = (j // 4, j % 4)
                for j, ui in enumerate(odd):
                    gslot[ui] = (2 + j // 4, j % 4)
                @blk.tensor
                def _(t):
                    for ui, (h, s_) in enumerate(units):
                        p, hp = h // 2, 64 * (h % 2)
                        ka_ = (kA if s_ == 0 else kAR)[hp:hp + 64, p * 128:(p + 1) * 128]
                        ki_ = (kIA if s_ == 0 else kIAR)[hp:hp + 64, p * 128:(p + 1) * 128]
                        bk, slot = gslot[ui]
                        t.matmul(pbf[:, bk * 512 + slot * 128: bk * 512 + (slot + 1) * 128],
                                 ka_, ki_, start=True, stop=True).then_inc(st_.h, 1)
                    st_.n += 16
                @blk.vector
                def _(v):
                    for ui, (h, s_) in enumerate(units):
                        bk, slot = gslot[ui]
                        v.wait_ge(st_.h, tb + 16)
                        sccol = scl[:, (NBRN2 if s_ == 0 else NGRN2) + h:
                                    (NBRN2 if s_ == 0 else NGRN2) + h + 1]
                        v.scalar_tensor_tensor(nbt[:, ui * 128:(ui + 1) * 128],
                                               pbf[:, bk * 512 + slot * 128: bk * 512 + (slot + 1) * 128],
                                               sccol, mLS[:, :], ALU.mult, ALU.mult
                                               ).then_inc(sv.h, 1)
                    sv.n += 16
            if not (stages2 & 128):
                continue
            tb, vb, sb0 = st_.n, sv.n, ss.n
            with nc.Block() as blk:
                @blk.tensor
                def _(t):
                    for w in range(2):
                        for k in range(8):
                            ui = w * 8 + k
                            if w > 0:
                                t.wait_ge(ss.h, sb0 + 8)
                            t.transpose(pb16[:, k * 128:(k + 1) * 128],
                                        nbt[:, ui * 128:(ui + 1) * 128], sid[:, :]).then_inc(st_.h, 1)
                    st_.n += 16
                @blk.scalar
                def _(sc):
                    for w in range(2):
                        sc.wait_ge(st_.h, tb + 8 * (w + 1))
                        for k in range(8):
                            ui = w * 8 + k
                            sc.copy(nbtT[:, ui * 128:(ui + 1) * 128],
                                    pb16[:, k * 128:(k + 1) * 128]).then_inc(ss.h, 1)
                    ss.n += 16
                @blk.vector
                def _(v):
                    for ui in range(16):
                        v.wait_ge(ss.h, sb0 + ui + 1)
                        v.tensor_tensor(Tc[:, ui * 128:(ui + 1) * 128],
                                        nbtT[:, ui * 128:(ui + 1) * 128], i16[:, :], ALU.add
                                        ).then_inc(sv.h, 1)
                    sv.n += 16
            prev, prevT = nbt, nbtT
            curs = [(pwA, pwAT), (pwB, pwBT)]
            for lv in range(1, 7):
                if not (stages2 & 16):
                    break
                cur, curT = curs[(lv - 1) % 2]
                tb, vb, sb0 = st_.n, sv.n, ss.n
                with nc.Block() as blk:
                    @blk.tensor
                    def _(t, prev=prev, prevT=prevT, cur=cur, curT=curT, lv=lv):
                        for ui in range(16):
                            bk, slot = divmod(ui, 4)
                            t.matmul(pbf[:, bk * 512 + slot * 128: bk * 512 + (slot + 1) * 128],
                                     prevT[:, ui * 128:(ui + 1) * 128],
                                     prev[:, ui * 128:(ui + 1) * 128],
                                     start=True, stop=True).then_inc(st_.h, 1)
                        if lv < 6:
                            for ui in range(16):
                                bk = 4 + ui // 8 if ui < 16 else 0
                                if ui < 8:
                                    bk, slot = 4 + ui // 4, ui % 4
                                else:
                                    bk, slot = (ui - 8) // 4, (ui - 8) % 4
                                    t.wait_ge(ss.h, sb0 + 8 + (4 if ui >= 12 else 0))
                                t.matmul(pbf[:, bk * 512 + slot * 128: bk * 512 + (slot + 1) * 128],
                                         prev[:, ui * 128:(ui + 1) * 128],
                                         prevT[:, ui * 128:(ui + 1) * 128],
                                         start=True, stop=True).then_inc(st_.h, 1)
                    @blk.scalar
                    def _(sc, cur=cur, curT=curT, lv=lv):
                        for bk in range(4):
                            sc.wait_ge(st_.h, tb + (bk + 1) * 4)
                            sc.copy(cur[:, bk * 512:(bk + 1) * 512],
                                    pbf[:, bk * 512:(bk + 1) * 512]).then_inc(ss.h, 4)
                        if lv < 6:
                            for j in range(2):
                                sc.wait_ge(st_.h, tb + 16 + (j + 1) * 4)
                                sc.copy(curT[:, j * 512:(j + 1) * 512],
                                        pbf[:, (4 + j) * 512:(5 + j) * 512]).then_inc(ss.h, 4)
                            for j in range(2):
                                sc.wait_ge(st_.h, tb + 24 + (j + 1) * 4)
                                sc.copy(curT[:, (2 + j) * 512:(3 + j) * 512],
                                        pbf[:, j * 512:(j + 1) * 512]).then_inc(ss.h, 4)
                    ss.n += 16 if lv == 6 else 32
                    st_.n += 16 if lv == 6 else 32
                if not (stages2 & 32):
                    prev, prevT = cur, curT
                    continue
                tb, vb, sb0 = st_.n, sv.n, ss.n
                with nc.Block() as blk:
                    @blk.tensor
                    def _(t, cur=cur):
                        for w in range(2):
                            for k in range(8):
                                ui = w * 8 + k
                                bk, slot = 4 + k // 4, k % 4
                                if w > 0:
                                    t.wait_ge(sv.h, vb + 8)
                                t.matmul(pbf[:, bk * 512 + slot * 128: bk * 512 + (slot + 1) * 128],
                                         cur[:, ui * 128:(ui + 1) * 128],
                                         Tc[:, ui * 128:(ui + 1) * 128],
                                         start=True, stop=True).then_inc(st_.h, 1)
                        st_.n += 16
                    @blk.vector
                    def _(v):
                        for w in range(2):
                            for k in range(8):
                                ui = w * 8 + k
                                bk, slot = 4 + k // 4, k % 4
                                v.wait_ge(st_.h, tb + 8 * (w + 1))
                                v.tensor_tensor(Tc[:, ui * 128:(ui + 1) * 128],
                                                pbf[:, bk * 512 + slot * 128: bk * 512 + (slot + 1) * 128],
                                                Tc[:, ui * 128:(ui + 1) * 128], ALU.add
                                                ).then_inc(sv.h, 1)
                        sv.n += 16
                prev, prevT = cur, curT
            if not (stages2 & 64):
                continue
            tb, vb, sb0 = st_.n, sv.n, ss.n
            with nc.Block() as blk:
                heads_half = [[0, 2, 4, 6], [1, 3, 5, 7]]
                @blk.tensor
                def _(t):
                    for half in range(2):
                        if half == 1:
                            t.wait_ge(sv.h, vb + 12)
                        k = 0
                        for h in heads_half[half]:
                            p, hp = h // 2, 64 * (h % 2)
                            for (gdst, lh, rh) in ((gq, kIA, qA), (gqR, kIAR, qAR), (gpre, kIA, kPRE)):
                                bk, slot = divmod(k, 4)
                                t.matmul(pbf[:, bk * 512 + slot * 128: bk * 512 + (slot + 1) * 128],
                                         lh[hp:hp + 64, p * 128:(p + 1) * 128],
                                         rh[hp:hp + 64, p * 128:(p + 1) * 128],
                                         start=True, stop=True).then_inc(st_.h, 1)
                                k += 1
                    st_.n += 24
                @blk.vector
                def _(v):
                    for half in range(2):
                        v.wait_ge(st_.h, tb + 12 * (half + 1))
                        k = 0
                        for h in heads_half[half]:
                            for (gdst, msk) in ((gq, mUI), (gqR, mUI), (gpre, mUS)):
                                bk, slot = divmod(k, 4)
                                v.tensor_tensor(gdst[:, h * 128:(h + 1) * 128],
                                                pbf[:, bk * 512 + slot * 128: bk * 512 + (slot + 1) * 128],
                                                msk[:, :], ALU.mult).then_inc(sv.h, 1)
                                k += 1
                    sv.n += 24

            if not (stages2 & 4):
                continue
            tb, vb, sb0 = st_.n, sv.n, ss.n
            with nc.Block() as blk:
                @blk.tensor
                def _(t):
                    nmm = 0
                    for h in range(8):
                        p, hp = h // 2, 64 * (h % 2)
                        xbk, xof = h % 2, (h // 2) * 64
                        t.matmul(pbf[:, xbk * 512 + xof: xbk * 512 + xof + 64],
                                 kA[hp:hp + 64, p * 128:(p + 1) * 128],
                                 spr[hp:hp + 64, p * 128:p * 128 + 64],
                                 start=True, stop=True).then_inc(st_.h, 1)
                        nmm += 1
                    for h in range(8):
                        ui = h * 2
                        t.wait_ge(sv.h, vb + h + 1)
                        t.matmul(pbf[:, 2 * 512 + h * 64: 2 * 512 + (h + 1) * 64],
                                 Tc[:, ui * 128:(ui + 1) * 128],
                                 xs[:, h * 64:(h + 1) * 64],
                                 start=True, stop=True).then_inc(st_.h, 1)
                        nmm += 1
                    for h in range(8):
                        p, hp = h // 2, 64 * (h % 2)
                        xbk, xof = 3 + h % 2, (h // 2) * 64
                        t.wait_ge(ss.h, sb0 + 1)
                        t.matmul(pbf[:, xbk * 512 + xof: xbk * 512 + xof + 64],
                                 kPRE[hp:hp + 64, p * 128:(p + 1) * 128],
                                 spr[hp:hp + 64, p * 128:p * 128 + 64],
                                 start=True, stop=False)
                        t.matmul(pbf[:, xbk * 512 + xof: xbk * 512 + xof + 64],
                                 gpre[:, h * 128:(h + 1) * 128],
                                 ub[:, h * 64:(h + 1) * 64],
                                 start=False, stop=True).then_inc(st_.h, 1)
                        nmm += 1
                    for h in range(8):
                        p, hp = h // 2, 64 * (h % 2)
                        xbk, xof = h % 2, (h // 2) * 64
                        t.wait_ge(sv.h, vb + 11 + 3 * h)
                        t.matmul(pbf[:, xbk * 512 + xof: xbk * 512 + xof + 64],
                                 kAR[hp:hp + 64, p * 128:(p + 1) * 128],
                                 spr[hp:hp + 64, p * 128 + 64:p * 128 + 128],
                                 start=True, stop=True).then_inc(st_.h, 1)
                        nmm += 1
                    for h in range(8):
                        ui = h * 2 + 1
                        t.wait_ge(sv.h, vb + 33 + h)
                        t.matmul(pbf[:, 2 * 512 + h * 64: 2 * 512 + (h + 1) * 64],
                                 Tc[:, ui * 128:(ui + 1) * 128],
                                 xs[:, h * 64:(h + 1) * 64],
                                 start=True, stop=True).then_inc(st_.h, 1)
                        nmm += 1
                    for h in range(8):
                        p, hp = h // 2, 64 * (h % 2)
                        t.wait_ge(ss.h, sb0 + 2)
                        t.matmul(pbf[:, 5 * 512 + h * 64: 5 * 512 + (h + 1) * 64],
                                 gq[:, h * 128:(h + 1) * 128],
                                 ub[:, h * 64:(h + 1) * 64],
                                 start=True, stop=False)
                        t.matmul(pbf[:, 5 * 512 + h * 64: 5 * 512 + (h + 1) * 64],
                                 qA[hp:hp + 64, p * 128:(p + 1) * 128],
                                 spr[hp:hp + 64, p * 128:p * 128 + 64],
                                 start=False, stop=False)
                        t.matmul(pbf[:, 5 * 512 + h * 64: 5 * 512 + (h + 1) * 64],
                                 gqR[:, h * 128:(h + 1) * 128],
                                 ubR[:, h * 64:(h + 1) * 64],
                                 start=False, stop=False)
                        t.matmul(pbf[:, 5 * 512 + h * 64: 5 * 512 + (h + 1) * 64],
                                 qAR[hp:hp + 64, p * 128:(p + 1) * 128],
                                 spr[hp:hp + 64, p * 128 + 64:p * 128 + 128],
                                 start=False, stop=True).then_inc(st_.h, 1)
                        nmm += 1
                    t.wait_ge(sv.h, vb + 32)
                    for h in range(8):
                        p, hp = h // 2, 64 * (h % 2)
                        sbk = 3 + (p % 2)
                        t.matmul(pbf[hp:hp + 64, sbk * 512 + (p // 2) * 256: sbk * 512 + (p // 2) * 256 + 64],
                                 kiat[:, p * 128 + 64 * (h % 2): p * 128 + 64 * (h % 2) + 64],
                                 ub[:, h * 64:(h + 1) * 64],
                                 start=True, stop=True).then_inc(st_.h, 1)
                        t.matmul(pbf[hp:hp + 64, sbk * 512 + (p // 2) * 256 + 64: sbk * 512 + (p // 2) * 256 + 128],
                                 kiart[:, p * 128 + 64 * (h % 2): p * 128 + 64 * (h % 2) + 64],
                                 ubR[:, h * 64:(h + 1) * 64],
                                 start=True, stop=True).then_inc(st_.h, 1)
                        nmm += 2
                    st_.n += nmm
                @blk.vector
                def _(v):
                    def vw(inst):
                        inst.then_inc(sv.h, 1); sv.n += 1; v.wait_ge(sv.h, sv.n)
                    for h in range(8):
                        xbk, xof = h % 2, (h // 2) * 64
                        v.wait_ge(st_.h, tb + 8)
                        v.scalar_tensor_tensor(xs[:, h * 64:(h + 1) * 64],
                                               pbf[:, xbk * 512 + xof: xbk * 512 + xof + 64],
                                               scl[:, NBRN2 + h:NBRN2 + h + 1],
                                               vsb[:, h * 64:(h + 1) * 64],
                                               ALU.mult, ALU.add).then_inc(sv.h, 1)
                    sv.n += 8
                    for h in range(8):
                        xbk, xof = 3 + h % 2, (h // 2) * 64
                        v.wait_ge(st_.h, tb + 24)
                        v.scalar_tensor_tensor(rtmp[:, h * 64:(h + 1) * 64],
                                               pbf[:, xbk * 512 + xof: xbk * 512 + xof + 64],
                                               scl[:, NRN + h:NRN + h + 1],
                                               vsl(h), ALU.mult, ALU.add).then_inc(sv.h, 1)
                        sv.n += 1
                        v.wait_ge(sv.h, sv.n)
                        v.tensor_scalar(rtmp[:, h * 64:(h + 1) * 64], rtmp[:, h * 64:(h + 1) * 64],
                                        -1.0, 1.0, ALU.max, ALU.min).then_inc(sv.h, 1)
                        sv.n += 1
                        v.wait_ge(sv.h, sv.n)
                        v.tensor_scalar(rs[:, h * 64:(h + 1) * 64], rtmp[:, h * 64:(h + 1) * 64],
                                        scl[:, GRN + h:GRN + h + 1], None, ALU.mult).then_inc(sv.h, 1)
                        sv.n += 1
                    for h in range(8):
                        xbk, xof = h % 2, (h // 2) * 64
                        v.wait_ge(st_.h, tb + 32)
                        v.scalar_tensor_tensor(xs[:, h * 64:(h + 1) * 64],
                                               pbf[:, xbk * 512 + xof: xbk * 512 + xof + 64],
                                               scl[:, NGRN2 + h:NGRN2 + h + 1],
                                               rs[:, h * 64:(h + 1) * 64],
                                               ALU.mult, ALU.add).then_inc(sv.h, 1)
                    sv.n += 8
                    for p in range(4):
                        sbk = 3 + (p % 2)
                        v.wait_ge(st_.h, tb + 64)
                        vw(v.tensor_tensor(spr[:, p * 128:(p + 1) * 128],
                                           pbf[:, sbk * 512 + (p // 2) * 256: sbk * 512 + (p // 2) * 256 + 128],
                                           spr[:, p * 128:(p + 1) * 128], ALU.add))
                        v.tensor_scalar(spr[:, p * 128:p * 128 + 64], spr[:, p * 128:p * 128 + 64],
                                        eclA[:, p:p + 1], None, ALU.mult)
                        vw(v.tensor_scalar(spr[:, p * 128 + 64:p * 128 + 128],
                                           spr[:, p * 128 + 64:p * 128 + 128],
                                           eclR[:, p:p + 1], None, ALU.mult))
                @blk.scalar
                def _(sc):
                    sc.wait_ge(st_.h, tb + 16)
                    sc.copy(ub[:, :], pbf[:, 2 * 512:3 * 512]).then_inc(ss.h, 1)
                    sc.wait_ge(st_.h, tb + 40)
                    sc.copy(ubR[:, :], pbf[:, 2 * 512:3 * 512]).then_inc(ss.h, 1)
                    sc.wait_ge(st_.h, tb + 48)
                    sc.copy(oT[:, :], pbf[:, 5 * 512:6 * 512]).then_inc(ss.h, 1)
                    ss.n += 3

            if not (stages2 & 8):
                continue
            tb, vb, sb0 = st_.n, sv.n, ss.n
            with nc.Block() as blk:
                @blk.tensor
                def _(t):
                    for p in range(4):
                        t.transpose(pb16[:, p * 128:(p + 1) * 128],
                                    oT[:, p * 128:(p + 1) * 128], sid[:, :]).then_inc(st_.h, 1)
                    st_.n += 4
                @blk.scalar
                def _(sc):
                    sc.wait_ge(st_.h, tb + 4)
                    for p in range(4):
                        sc.copy(ofsl[:, p * 512 + cis * 128: p * 512 + (cis + 1) * 128],
                                pb16[:, p * 128:(p + 1) * 128]).then_inc(ss.h, 1)
                    ss.n += 4
            if cis == 3:
                t0 = seg * 512
                sh, tloc = divmod(t0, T // 2)
                with nc.Block() as blk:
                    @blk.sync
                    def _(s, sh=sh, tloc=tloc):
                        for p in range(4):
                            so.inc(s.dma_start(ofd[sh * 4 + p, :, tloc:tloc + 512],
                                               ofsl[:, p * 512:(p + 1) * 512]), 16)
                        s.wait_ge(so.h, so.n)
    return nc


def build_phase3(nc, stages3=255, ext=None, pfx="", fused=False):
    TH = T // 2
    NG = TH // 512
    if ext is None:
        ext = {}
        for nm, shape, dt in [("x3", [TH, DM], F32), ("ofA", [4, 128, TH], BF16),
                              ("ofB", [4, 128, TH], BF16), ("woT", [DM, DM], F32),
                              ("abd", [TH // 128, 128, 2], F32), ("wr", [DM, 8], F32),
                              ("brv", [8], F32), ("w1", [8, DM, 2048], F32),
                              ("b1v", [8, 2048], F32), ("w2", [8, 2048, DM], F32),
                              ("b2v", [8, DM], F32), ("id16", [128, 128], BF16),
                              ("seld", [8, 8 * 128], BF16)]:
            ext[nm] = nc.dram_tensor(nm, shape, dt, kind="ExternalInput").ap()
        ext["out"] = nc.dram_tensor("out", [TH, DM], F32, kind="ExternalOutput").ap()
    x, woT, wr, brv = (ext[n] for n in ("x3", "woT", "wr", "brv"))
    w1, b1v, w2, b2v, id16, seld, out = (ext[n] for n in
        ("w1", "b1v", "w2", "b2v", "id16", "seld", "out"))
    if fused:
        ofg, abg, coreid = ext["ofg"], ext["abg"], ext["coreid"]
        ofA = ofB = abd = None
    else:
        ofA, ofB, abd = ext["ofA"], ext["ofB"], ext["abd"]
    x1d = nc.dram_tensor(pfx + "x1d", [TH, DM], F32)
    xn3d = nc.dram_tensor(pfx + "xn3d", [8, 128, TH], BF16)

    with contextlib.ExitStack() as es:
        sb = lambda n, s, d: es.enter_context(nc.sbuf_tensor(pfx + n, s, d))
        ps = lambda n, s, d: es.enter_context(nc.psum_tensor(pfx + n, s, d))
        mksem = lambda n: Ctr(es.enter_context(nc.semaphore(pfx + n)))

        sid = sb("sid", [128, 128], BF16)
        onesr = sb("onesr", [1, 512], BF16)
        wld3 = sb("wld3", [128, 2048], F32)
        wr16 = sb("wr16", [128, 64], BF16)
        brbc = sb("brbc", [128, 8], F32)
        abt = sb("abt", [128, (TH // 128) * 2], F32)
        stat3 = sb("stat3", [128, 64], F32)
        mp16 = sb("mp16", [8, TH], BF16)
        sel8 = sb("sel8", [8, 8 * 128], BF16)
        lgt = sb("lgt", [128, 8], F32)
        ecol = sb("ecol", [128, 8], F32)
        mcol = sb("mcol", [128, 8], F32)

        pbf = ps("pbf", [128, 3072], F32)
        pb16 = ps("pb16", [128, 2048], BF16)
        bank = lambda i: pbf[:, 512 * i:512 * (i + 1)]

        sd = mksem("sd")
        st_ = mksem("st_")
        sv = mksem("sv")
        ss = mksem("ss")
        so = mksem("so")
        SX3, SS3, NM3, S33, TM3, SC3 = 0, 8, 16, 24, 32, 40

        cidb = sb("cidb", [1, 1], mybir.dt.int32) if fused else None
        with nc.Block() as blk:
            @blk.sync
            def _(s):
                sd.inc(s.dma_start(sid[:, :], id16), 16)
                if not fused:
                    for tt in range(TH // 128):
                        sd.inc(s.dma_start(abt[:, tt * 2:(tt + 1) * 2], abd[tt, :, :]), 16)
                else:
                    sd.inc(s.dma_start(cidb[:, :], coreid), 16)
                for kc in range(8):
                    sd.inc(s.dma_start(wld3[:, 1024 + kc * 8:1024 + (kc + 1) * 8],
                                       wr[kc * 128:(kc + 1) * 128, :]), 16)
                sd.inc(s.dma_start(wld3[0:1, 0:8], brv), 16)
                sd.inc(s.dma_start(sel8[:, :], seld), 16)
                s.wait_ge(sd.h, sd.n)
        if fused:
            with nc.Block() as blk:
                @blk.gpsimd
                def _(g):
                    with g.register("cidr") as cidr:
                        g.reg_load(cidr, cidb[0:1, 0:1])
                        for c in range(8):
                            hh = c % 2
                            with g.If_eq(cidr, c):
                                for tt in range(TH // 128):
                                    sd.inc(g.dma_start(
                                        abt[:, tt * 2:(tt + 1) * 2],
                                        abg[c * 64 + hh * 32 + tt, :, :]), 16)
                                sd.n -= 16 * (TH // 128)
                        sd.n += 16 * (TH // 128)
                        g.wait_ge(sd.h, sd.n)
        with nc.Block() as blk:
            @blk.vector
            def _(v):
                v.memset(onesr[:, :], 1.0)
                v.memset(wld3[0:1, 8:136], 1.0)
                v.tensor_copy(wr16[:, :], wld3[:, 1024:1088])
        with nc.Block() as blk:
            @blk.tensor
            def _(t):
                t.matmul(pbf[:, 0:8], wld3[0:1, 8:136], wld3[0:1, 0:8],
                         start=True, stop=True).then_inc(st_.h, 1)
                st_.n += 1
            @blk.vector
            def _(v):
                v.wait_ge(st_.h, st_.n)
                v.tensor_copy(brbc[:, :], pbf[:, 0:8])

        if not (stages3 & 1):
            NGa = 0
        else:
            NGa = NG
        with contextlib.ExitStack() as e3:
            sba = lambda n, s, d: e3.enter_context(nc.sbuf_tensor(n, s, d))
            wo16 = sba("wo16", [128, 8 * 1024], BF16)
            ofs = sba("ofs", [128, 8 * 512], BF16)
            xtok = sba("xtok", [128, 4 * 1024], F32)
            x1t = sba("x1t", [128, 4 * 1024], F32)
            attn16 = sba("attn16", [128, 8 * 512], BF16)
            xz3 = sba("xz3", [128, 1024], BF16)
            xn3g = sba("xn3g", [128, 8 * 512], BF16)
            mprow = sba("mprow", [32, TH], F32)
            mps = sba("mps", [128, 32], F32)
            for ic in range(8):
                with nc.Block() as blk:
                    @blk.sync
                    def _(s, ic=ic):
                        sd.inc(s.dma_start(wld3[:, 0:1024], woT[ic * 128:(ic + 1) * 128, :]), 16)
                        s.wait_ge(sd.h, sd.n)
                with nc.Block() as blk:
                    @blk.vector
                    def _(v, ic=ic):
                        v.tensor_copy(wo16[:, ic * 1024:(ic + 1) * 1024], wld3[:, 0:1024])
            with nc.Block() as blk:
                @blk.vector
                def _(v):
                    v.memset(mps[:, :], 0.0)

            for g in range(NGa):
                t0 = g * 512
                with nc.Block() as blk:
                    if fused:
                        @blk.gpsimd
                        def _(g, t0=t0):
                            with g.register(f"cidr2_{t0}") as cidr:
                                g.reg_load(cidr, cidb[0:1, 0:1])
                                for c in range(8):
                                    b_, hh = c // 2, c % 2
                                    with g.If_eq(cidr, c):
                                        for ic in range(4):
                                            sd.inc(g.dma_start(
                                                ofs[:, ic * 512:(ic + 1) * 512],
                                                ofg[16 * b_ + hh * 4 + ic, :, t0:t0 + 512]), 16)
                                            sd.inc(g.dma_start(
                                                ofs[:, (4 + ic) * 512:(5 + ic) * 512],
                                                ofg[16 * b_ + 8 + hh * 4 + ic, :, t0:t0 + 512]), 16)
                                        sd.n -= 16 * 8
                                sd.n += 16 * 8
                    @blk.sync
                    def _(s, t0=t0):
                        if not fused:
                            for ic in range(4):
                                sd.inc(s.dma_start(ofs[:, ic * 512:(ic + 1) * 512],
                                                   ofA[ic, :, t0:t0 + 512]), 16)
                                sd.inc(s.dma_start(ofs[:, (4 + ic) * 512:(5 + ic) * 512],
                                                   ofB[ic, :, t0:t0 + 512]), 16)
                        for ts in range(4):
                            sd.inc(s.dma_start(xtok[:, ts * 1024:(ts + 1) * 1024],
                                               x[t0 + ts * 128: t0 + (ts + 1) * 128, :]), 16)
                        s.wait_ge(sd.h, sd.n)
                tb, vb, sb0 = st_.n, sv.n, ss.n
                with nc.Block() as blk:
                    @blk.tensor
                    def _(t):
                        for dc in range(8):
                            bk = dc % 2
                            if dc >= 2:
                                t.wait_ge(ss.h, sb0 + dc - 2 + 1)
                            mm = None
                            for ic in range(8):
                                mm = t.matmul(bank(bk)[:, 0:512],
                                              wo16[:, ic * 1024 + dc * 128: ic * 1024 + (dc + 1) * 128],
                                              ofs[:, ic * 512:(ic + 1) * 512],
                                              start=(ic == 0), stop=(ic == 7))
                            mm.then_inc(st_.h, 1)
                        st_.n += 8
                    @blk.scalar
                    def _(sc):
                        for dc in range(8):
                            bk = dc % 2
                            sc.wait_ge(st_.h, tb + dc + 1)
                            sc.copy(attn16[:, dc * 512:(dc + 1) * 512],
                                    bank(bk)[:, 0:512]).then_inc(ss.h, 1)
                        ss.n += 8
                tb, vb, sb0 = st_.n, sv.n, ss.n
                with nc.Block() as blk:
                    @blk.tensor
                    def _(t):
                        for ts in range(4):
                            if ts >= 2:
                                t.wait_ge(sv.h, vb + 2 * (ts - 2) + 2)
                            for dc in range(8):
                                t.transpose(pb16[:, (ts % 2) * 1024 + dc * 128:
                                            (ts % 2) * 1024 + (dc + 1) * 128],
                                            attn16[:, dc * 512 + ts * 128: dc * 512 + (ts + 1) * 128],
                                            sid[:, :]).then_inc(st_.h, 1)
                        st_.n += 32
                    @blk.vector
                    def _(v):
                        for ts in range(4):
                            tt = g * 4 + ts
                            v.tensor_scalar(x1t[:, ts * 1024:(ts + 1) * 1024],
                                            xtok[:, ts * 1024:(ts + 1) * 1024],
                                            abt[:, tt * 2:tt * 2 + 1],
                                            abt[:, tt * 2 + 1:tt * 2 + 2],
                                            ALU.mult, ALU.add).then_inc(sv.h, 1)
                            sv.n += 1
                            v.wait_ge(sv.h, sv.n)
                            v.wait_ge(st_.h, tb + 8 * (ts + 1))
                            v.tensor_tensor(x1t[:, ts * 1024:(ts + 1) * 1024],
                                            pb16[:, (ts % 2) * 1024:(ts % 2) * 1024 + 1024],
                                            x1t[:, ts * 1024:(ts + 1) * 1024],
                                            ALU.add).then_inc(sv.h, 1)
                            sv.n += 1
                with nc.Block() as blk:
                    @blk.scalar
                    def _(sc):
                        for ts in range(4):
                            sc.activation(wld3[:, 0:1024], x1t[:, ts * 1024:(ts + 1) * 1024],
                                          AF.Square, accum_out=stat3[:, SS3 + ts:SS3 + ts + 1])
                    @blk.vector
                    def _(v):
                        for ts in range(4):
                            v.reduce_sum(stat3[:, SX3 + ts:SX3 + ts + 1],
                                         x1t[:, ts * 1024:(ts + 1) * 1024], axis=AXX)
                A3 = lambda c: stat3[:, c:c + 4]
                with nc.Block() as blk:
                    @blk.vector
                    def _(v):
                        def vw(inst):
                            inst.then_inc(sv.h, 1); sv.n += 1; v.wait_ge(sv.h, sv.n)
                        v.tensor_scalar(A3(NM3), A3(SX3), -1.0 / DM, None, ALU.mult)
                        vw(v.tensor_scalar(A3(SC3), A3(SS3), 1.0 / DM, None, ALU.mult))
                        vw(v.tensor_tensor(A3(TM3), A3(NM3), A3(NM3), ALU.mult))
                        vw(v.tensor_tensor(A3(SC3), A3(SC3), A3(TM3), ALU.subtract))
                        vw(v.tensor_scalar(A3(TM3), A3(SC3), EPS, None, ALU.add))
                with nc.Block() as blk:
                    @blk.scalar
                    def _(sc):
                        sc.activation(A3(S33), A3(TM3), AF.Ln).then_inc(ss.h, 1)
                        ss.n += 1
                        sc.wait_ge(ss.h, ss.n)
                        sc.activation(A3(S33), A3(S33), AF.Exp, scale=-0.5).then_inc(ss.h, 1)
                        ss.n += 1
                with nc.Block() as blk:
                    @blk.vector
                    def _(v):
                        def vw(inst):
                            inst.then_inc(sv.h, 1); sv.n += 1; v.wait_ge(sv.h, sv.n)
                        vw(v.tensor_tensor(A3(SC3), A3(S33), A3(S33), ALU.mult))
                        vw(v.tensor_tensor(A3(SC3), A3(SC3), A3(TM3), ALU.mult))
                        vw(v.tensor_scalar(A3(SC3), A3(SC3), -0.5, 1.5, ALU.mult, ALU.add))
                        vw(v.tensor_tensor(A3(S33), A3(S33), A3(SC3), ALU.mult))
                tb, vb, sb0 = st_.n, sv.n, ss.n
                with nc.Block() as blk:
                    @blk.vector
                    def _(v):
                        for ts in range(4):
                            if ts > 0:
                                v.wait_ge(st_.h, tb + 8 * ts)
                            v.tensor_scalar(xz3[:, :], x1t[:, ts * 1024:(ts + 1) * 1024],
                                            stat3[:, NM3 + ts:NM3 + ts + 1],
                                            stat3[:, S33 + ts:S33 + ts + 1],
                                            ALU.add, ALU.mult).then_inc(sv.h, 1)
                            sv.n += 1
                    @blk.tensor
                    def _(t):
                        for ts in range(4):
                            t.wait_ge(sv.h, vb + ts + 1)
                            if ts >= 2:
                                t.wait_ge(ss.h, sb0 + 8 * (ts - 1))
                            for dc in range(8):
                                t.transpose(pb16[:, (ts % 2) * 1024 + dc * 128:
                                            (ts % 2) * 1024 + (dc + 1) * 128],
                                            xz3[:, dc * 128:(dc + 1) * 128],
                                            sid[:, :]).then_inc(st_.h, 1)
                        st_.n += 32
                    @blk.scalar
                    def _(sc):
                        for ts in range(4):
                            sc.wait_ge(st_.h, tb + 8 * (ts + 1))
                            for dc in range(8):
                                sc.copy(xn3g[:, dc * 512 + ts * 128: dc * 512 + (ts + 1) * 128],
                                        pb16[:, (ts % 2) * 1024 + dc * 128:
                                             (ts % 2) * 1024 + (dc + 1) * 128]).then_inc(ss.h, 1)
                        ss.n += 32
                tb, vb, sb0 = st_.n, sv.n, ss.n
                with nc.Block() as blk:
                    @blk.tensor
                    def _(t):
                        for ts in range(4):
                            if ts > 0:
                                t.wait_ge(sv.h, vb + 8 * (ts - 1) + 1)
                            mm = None
                            for kc in range(8):
                                mm = t.matmul(pbf[:, 2 * 512:2 * 512 + 8],
                                              xn3g[:, kc * 512 + ts * 128: kc * 512 + (ts + 1) * 128],
                                              wr16[:, kc * 8:(kc + 1) * 8],
                                              start=(kc == 0), stop=(kc == 7))
                            mm.then_inc(st_.h, 1)
                        st_.n += 4
                    @blk.vector
                    def _(v, t0=t0):
                        def vw(inst):
                            inst.then_inc(sv.h, 1); sv.n += 1; v.wait_ge(sv.h, sv.n)
                        for ts in range(4):
                            v.wait_ge(st_.h, tb + ts + 1)
                            vw(v.tensor_tensor(lgt[:, :], pbf[:, 2 * 512:2 * 512 + 8],
                                               brbc[:, :], ALU.add))
                            vw(v.reduce_max(mcol[:, 0:1], lgt[:, :], axis=AXX))
                            vw(v.tensor_scalar(mcol[:, 1:2], mcol[:, 0:1], -1.0, None, ALU.mult))
                            v.wait_ge(ss.h, sb0 + ts + 1)
                            vw(v.reduce_sum(mcol[:, 2:3], ecol[:, :], axis=AXX))
                            vw(v.reciprocal(mcol[:, 3:4], mcol[:, 2:3]))
                            vw(v.tensor_scalar(ecol[:, :], lgt[:, :], mcol[:, 0:1], None,
                                               ALU.is_ge))
                            vw(v.tensor_scalar(mps[:, 0:8], ecol[:, :], mcol[:, 3:4], None,
                                               ALU.mult))
                            lastt = None
                            for bi in range(4):
                                lastt = v.transpose(
                                    mprow[0:32, t0 + ts * 128 + 32 * bi: t0 + ts * 128 + 32 * (bi + 1)],
                                    mps[32 * bi:32 * (bi + 1), 0:32])
                            vw(lastt)
                    @blk.scalar
                    def _(sc):
                        for ts in range(4):
                            sc.wait_ge(sv.h, vb + 8 * ts + 3)
                            sc.activation(ecol[:, :], lgt[:, :], AF.Exp,
                                          bias=mcol[:, 1:2]).then_inc(ss.h, 1)
                        ss.n += 4
                with nc.Block() as blk:
                    @blk.sync
                    def _(s, t0=t0):
                        for ts in range(4):
                            so.inc(s.dma_start(x1d[t0 + ts * 128: t0 + (ts + 1) * 128, :],
                                               x1t[:, ts * 1024:(ts + 1) * 1024]), 16)
                        for dc in range(8):
                            so.inc(s.dma_start(xn3d[dc, :, t0:t0 + 512],
                                               xn3g[:, dc * 512:(dc + 1) * 512]), 16)
                        s.wait_ge(so.h, so.n)
            with nc.Block() as blk:
                @blk.vector
                def _(v):
                    v.tensor_copy(mp16[:, :], mprow[0:8, :])

        with contextlib.ExitStack() as eacc:
            acc = eacc.enter_context(nc.sbuf_tensor(pfx + "acc", [128, 8 * TH], BF16))
            with nc.Block() as blk:
                @blk.vector
                def _(v):
                    v.memset(acc[:, :], 0.0)
            with contextlib.ExitStack() as e3:
                sba = lambda n, s, d: e3.enter_context(nc.sbuf_tensor(n, s, d))
                w116 = sba("w116", [128, 8 * 2048], BF16)
                w216 = sba("w216", [128, 16 * 1024], BF16)
                hm16 = sba("hm16", [128, 16 * 512], BF16)
                xn3s = sba("xn3s", [128, 2 * 8 * 512], BF16)
                mbc = sba("mbc", [128, 512], BF16)
                b1r = sba("b1r", [1, 2048], BF16)
                b2t = sba("b2t", [8, 1024], BF16)
                with nc.Block() as blk:
                    @blk.sync
                    def _(s):
                        sd.inc(s.dma_start(wld3[0:8, 0:1024], b2v), 16)
                        s.wait_ge(sd.h, sd.n)
                with nc.Block() as blk:
                    @blk.vector
                    def _(v):
                        v.tensor_copy(b2t[:, :], wld3[0:8, 0:1024])
                for e in range(8 if (stages3 & 2) else 0):
                    for kc in range(8):
                        with nc.Block() as blk:
                            @blk.sync
                            def _(s, e=e, kc=kc):
                                sd.inc(s.dma_start(wld3[:, 0:2048],
                                                   w1[e, kc * 128:(kc + 1) * 128, :]), 16)
                                s.wait_ge(sd.h, sd.n)
                        with nc.Block() as blk:
                            @blk.vector
                            def _(v, kc=kc):
                                v.tensor_copy(w116[:, kc * 2048:(kc + 1) * 2048], wld3[:, 0:2048])
                    for kc in range(16):
                        with nc.Block() as blk:
                            @blk.sync
                            def _(s, e=e, kc=kc):
                                sd.inc(s.dma_start(wld3[:, 0:1024],
                                                   w2[e, kc * 128:(kc + 1) * 128, :]), 16)
                                s.wait_ge(sd.h, sd.n)
                        with nc.Block() as blk:
                            @blk.vector
                            def _(v, kc=kc):
                                v.tensor_copy(w216[:, kc * 1024:(kc + 1) * 1024], wld3[:, 0:1024])
                    with nc.Block() as blk:
                        @blk.sync
                        def _(s, e=e):
                            sd.inc(s.dma_start(wld3[0:1, 0:2048], b1v[e, :]), 16)
                            s.wait_ge(sd.h, sd.n)
                    with nc.Block() as blk:
                        @blk.vector
                        def _(v):
                            v.tensor_copy(b1r[:, :], wld3[0:1, 0:2048])
                    for g in range(NG):
                        t0 = g * 512
                        gb = g % 2
                        with nc.Block() as blk:
                            @blk.sync
                            def _(s, t0=t0, gb=gb):
                                for dc in range(8):
                                    sd.inc(s.dma_start(
                                        xn3s[:, gb * 4096 + dc * 512: gb * 4096 + (dc + 1) * 512],
                                        xn3d[dc, :, t0:t0 + 512]), 16)
                                s.wait_ge(sd.h, sd.n)
                        tb, vb, sb0 = st_.n, sv.n, ss.n
                        with nc.Block() as blk:
                            @blk.tensor
                            def _(t, e=e, g=g, gb=gb, t0=t0):
                                t.matmul(bank(4)[:, 0:512], sel8[:, e * 128:(e + 1) * 128],
                                         mp16[:, t0:t0 + 512],
                                         start=True, stop=True).then_inc(st_.h, 1)
                                for hc in range(16):
                                    bk = hc % 4
                                    if hc >= 4:
                                        t.wait_ge(sv.h, vb + hc - 4 + 1)
                                    for kc in range(8):
                                        t.matmul(bank(bk)[:, 0:512],
                                                 w116[:, kc * 2048 + hc * 128: kc * 2048 + (hc + 1) * 128],
                                                 xn3s[:, gb * 4096 + kc * 512: gb * 4096 + (kc + 1) * 512],
                                                 start=(kc == 0), stop=False)
                                    t.matmul(bank(bk)[:, 0:512], b1r[0:1, hc * 128:(hc + 1) * 128],
                                             onesr[0:1, 0:512],
                                             start=False, stop=True).then_inc(st_.h, 1)
                                for dc in range(8):
                                    bk = dc % 4
                                    t.wait_ge(sv.h, vb + 16 + max(0, dc))
                                    mm = None
                                    for kc in range(16):
                                        mm = t.matmul(bank(bk)[:, 0:512],
                                                      w216[:, kc * 1024 + dc * 128: kc * 1024 + (dc + 1) * 128],
                                                      hm16[:, kc * 512:(kc + 1) * 512],
                                                      start=(kc == 0),
                                                      stop=(kc == 15 and e != 7))
                                    if e == 7:
                                        mm = t.matmul(bank(bk)[:, 0:512],
                                                      b2t[:, dc * 128:(dc + 1) * 128],
                                                      mp16[:, t0:t0 + 512],
                                                      start=False, stop=True)
                                    mm.then_inc(st_.h, 1)
                                st_.n += 25
                            @blk.scalar
                            def _(sc):
                                sc.wait_ge(st_.h, tb + 1)
                                sc.copy(mbc[:, :], bank(4)[:, 0:512]).then_inc(ss.h, 1)
                                ss.n += 1
                            @blk.vector
                            def _(v, t0=t0):
                                for hc in range(16):
                                    bk = hc % 4
                                    v.wait_ge(st_.h, tb + 1 + hc + 1)
                                    v.wait_ge(ss.h, sb0 + 1)
                                    v.scalar_tensor_tensor(hm16[:, hc * 512:(hc + 1) * 512],
                                                           bank(bk)[:, 0:512], 0.0, mbc[:, :],
                                                           ALU.max, ALU.mult).then_inc(sv.h, 1)
                                    sv.n += 1
                                for dc in range(8):
                                    bk = dc % 4
                                    v.wait_ge(st_.h, tb + 17 + dc + 1)
                                    v.tensor_tensor(acc[:, dc * TH + t0: dc * TH + t0 + 512],
                                                    bank(bk)[:, 0:512],
                                                    acc[:, dc * TH + t0: dc * TH + t0 + 512],
                                                    ALU.add).then_inc(sv.h, 1)
                                    sv.n += 1
            with contextlib.ExitStack() as e3:
                sba = lambda n, s, d: e3.enter_context(nc.sbuf_tensor(n, s, d))
                x1b = sba("x1b", [128, 4 * 1024], F32)
                outb = sba("outb", [128, 4 * 1024], F32)
                for g in range(NG if (stages3 & 4) else 0):
                    t0 = g * 512
                    with nc.Block() as blk:
                        @blk.sync
                        def _(s, t0=t0):
                            for ts in range(4):
                                sd.inc(s.dma_start(x1b[:, ts * 1024:(ts + 1) * 1024],
                                                   x1d[t0 + ts * 128: t0 + (ts + 1) * 128, :]), 16)
                            s.wait_ge(sd.h, sd.n)
                    tb, vb, sb0 = st_.n, sv.n, ss.n
                    with nc.Block() as blk:
                        @blk.tensor
                        def _(t, t0=t0):
                            for ts in range(4):
                                if ts >= 2:
                                    t.wait_ge(sv.h, vb + ts - 2 + 1)
                                for dc in range(8):
                                    t.transpose(pb16[:, (ts % 2) * 1024 + dc * 128:
                                                (ts % 2) * 1024 + (dc + 1) * 128],
                                                acc[:, dc * TH + t0 + ts * 128: dc * TH + t0 + (ts + 1) * 128],
                                                sid[:, :]).then_inc(st_.h, 1)
                            st_.n += 32
                        @blk.vector
                        def _(v):
                            for ts in range(4):
                                v.wait_ge(st_.h, tb + 8 * (ts + 1))
                                v.tensor_tensor(outb[:, ts * 1024:(ts + 1) * 1024],
                                                pb16[:, (ts % 2) * 1024:(ts % 2) * 1024 + 1024],
                                                x1b[:, ts * 1024:(ts + 1) * 1024],
                                                ALU.add).then_inc(sv.h, 1)
                                sv.n += 1
                    with nc.Block() as blk:
                        @blk.sync
                        def _(s, t0=t0):
                            for ts in range(4):
                                so.inc(s.dma_start(out[t0 + ts * 128: t0 + (ts + 1) * 128, :],
                                                   outb[:, ts * 1024:(ts + 1) * 1024]), 16)
                            s.wait_ge(so.h, so.n)
    return nc


def _hostprep_phase3(inputs, core, ofds):
    b, hh = core // 2, core % 2
    f = np.float32
    cc = np.ascontiguousarray
    TH = T // 2
    t0 = hh * TH
    return {
        "x": cc(np.asarray(inputs["L0_attn_x"], f)[b, t0:t0 + TH]),
        "ofA": cc(ofds[2 * b][:, :, t0:t0 + TH]),
        "ofB": cc(ofds[2 * b + 1][:, :, t0:t0 + TH]),
        "woT": cc(np.asarray(inputs["L0_attn_Wo"], f).T),
        "wr": cc(np.asarray(inputs["L0_ffn_Wr"], f)),
        "brv": cc(np.asarray(inputs["L0_ffn_br"], f)),
        "w1": cc(np.asarray(inputs["L0_ffn_W1"], f)),
        "b1v": cc(np.asarray(inputs["L0_ffn_b1"], f)),
        "w2": cc(np.asarray(inputs["L0_ffn_W2"], f)),
        "b2v": cc(np.asarray(inputs["L0_ffn_b2"], f)),
        "id16": np.eye(128, dtype=BF).view(np.uint16),
    }


def _consts2():
    mks = np.concatenate([np.tril(np.ones((128, 128), BF), -1),
                          np.triu(np.ones((128, 128), BF), 1),
                          np.triu(np.ones((128, 128), BF), 0)], axis=1)
    return mks.view(np.uint16)


def _sel8():
    s = np.zeros((8, 8 * 128), dtype=BF)
    for e in range(8):
        s[e, e * 128:(e + 1) * 128] = 1
    return s.view(np.uint16)


def build_fused(nc):
    TH = T // 2
    NTl = T // 128
    ext = {}
    ins = [("x", [T, DM], F32), ("wqT", [DM, 512], F32), ("wkT", [DM, 512], F32),
           ("waT", [DM, 512], F32), ("waRT", [DM, 512], F32), ("wvbgT", [DM, 528], F32),
           ("bav", [512], F32), ("baRv", [512], F32), ("bbgv", [16], F32),
           ("lnwv", [DM], F32), ("lnbv", [DM], F32), ("id16", [128, 128], BF16),
           ("mks", [128, 3 * 128], BF16), ("x3", [TH, DM], F32), ("woT", [DM, DM], F32),
           ("wr", [DM, 8], F32), ("brv", [8], F32), ("w1", [8, DM, 2048], F32),
           ("b1v", [8, 2048], F32), ("w2", [8, 2048, DM], F32), ("b2v", [8, DM], F32),
           ("seld", [8, 8 * 128], BF16)]
    for nm, shape, dt in ins:
        ext[nm] = nc.dram_tensor(nm, shape, dt, kind="ExternalInput").ap()
    ext["coreid"] = nc.dram_tensor("coreid", [1, 1], mybir.dt.int32,
                                   kind="ExternalInput").ap()
    ext["out"] = nc.dram_tensor("out", [TH, DM], F32, kind="ExternalOutput").ap()
    intr = [("qd", [4, 128, T], BF16), ("kd", [4, 128, T], BF16),
            ("lad", [4, 128, T], F32), ("laRd", [4, 128, T], F32),
            ("vtd", [NTl, 128, 512], BF16), ("tgd", [NTl, 128, 26], F32)]
    for nm, shape, dt in intr:
        ext[nm] = nc.dram_tensor(nm, shape, dt).ap()
    abx_t = nc.dram_tensor("abx", [NTl, 128, 2], F32)
    ofd_t = nc.dram_tensor("ofd", [8, 128, TH], BF16)
    ofg_t = nc.dram_tensor("ofg", [64, 128, TH], BF16)
    abg_t = nc.dram_tensor("abg", [8 * NTl, 128, 2], F32)
    ext["abx"] = abx_t.ap()
    ext["ofd"] = ofd_t.ap()
    ext["ofg"] = ofg_t.ap()
    ext["abg"] = abg_t.ap()
    build_phase1(nc, ext=ext, pfx="p1_")
    build_phase2(nc, ext=ext, pfx="p2_")
    with nc.semaphore("ccsem") as ccsem:
        with nc.Block() as blk:
            @blk.gpsimd
            def _(g):
                g.collective_compute(
                    "AllGather", mybir.AluOpType.bypass,
                    replica_groups=[list(range(8))],
                    ins=[ofd_t.ap().opt()], outs=[ofg_t.ap().opt()],
                ).then_inc(ccsem, 1)
                g.collective_compute(
                    "AllGather", mybir.AluOpType.bypass,
                    replica_groups=[list(range(8))],
                    ins=[abx_t.ap().opt()], outs=[abg_t.ap().opt()],
                ).then_inc(ccsem, 1)
                g.wait_ge(ccsem, 2)
        build_phase3(nc, ext=ext, pfx="p3_", fused=True)
    return nc


def _hostprep_fused(inputs, core):
    m = _hostprep_phase1(inputs, core)
    b, hh = core // 2, core % 2
    TH = T // 2
    f = np.float32
    cc = np.ascontiguousarray
    m["mks"] = _consts2()
    m["x3"] = cc(np.asarray(inputs["L0_attn_x"], f)[b, hh * TH:(hh + 1) * TH])
    m["woT"] = cc(np.asarray(inputs["L0_attn_Wo"], f).T)
    m["wr"] = cc(np.asarray(inputs["L0_ffn_Wr"], f))
    m["brv"] = cc(np.asarray(inputs["L0_ffn_br"], f))
    m["w1"] = cc(np.asarray(inputs["L0_ffn_W1"], f))
    m["b1v"] = cc(np.asarray(inputs["L0_ffn_b1"], f))
    m["w2"] = cc(np.asarray(inputs["L0_ffn_W2"], f))
    m["b2v"] = cc(np.asarray(inputs["L0_ffn_b2"], f))
    m["seld"] = _sel8()
    m["coreid"] = np.array([[core]], dtype=np.int32)
    return m


def kernel(**inputs):
    nc = bass.Bass("TRN2", target_bir_lowering=False)
    build_fused(nc)
    im = [_hostprep_fused(inputs, c) for c in range(8)]
    res = run_bass_kernel_spmd(nc, im, core_ids=list(range(8)))
    out = np.zeros((B, T, DM), np.float32)
    TH = T // 2
    for c in range(8):
        b, hh = c // 2, c % 2
        out[b, hh * TH:(hh + 1) * TH] = np.asarray(res.results[c]["out"])
    return out


# version 03
# speedup vs baseline: 1.2262x; 1.2262x over previous
import contextlib
import numpy as np
import ml_dtypes
import concourse.bass as bass
import concourse.mybir as mybir
from concourse.bass_utils import run_bass_kernel_spmd

F32 = mybir.dt.float32
BF16 = mybir.dt.bfloat16
AF = mybir.ActivationFunctionType
ALU = mybir.AluOpType
BF = ml_dtypes.bfloat16
AXX = mybir.AxisListType.X

B, T, DM, H, DH = 4, 8192, 1024, 16, 64
C = 128
NCH = T // C
SEG1 = 1024
EPS = 1e-5
NT = T // 128


class Ctr:
    def __init__(self, h):
        self.h = h
        self.n = 0
    def inc(self, inst, k=1):
        inst.then_inc(self.h, k)
        self.n += k
        return self.n


def build_phase1(nc, debug=False, stages=255, ext=None, pfx=""):
    NTl = T // 128
    if ext is None:
        ext = {}
        for nm, shape, dt in [("x", [T, DM], F32), ("wqT", [DM, 512], F32),
                              ("wkT", [DM, 512], F32), ("waT", [DM, 512], F32),
                              ("waRT", [DM, 512], F32), ("wvbgT", [DM, 528], F32),
                              ("bav", [512], F32), ("baRv", [512], F32),
                              ("bbgv", [16], F32), ("lnwv", [DM], F32),
                              ("lnbv", [DM], F32), ("id16", [128, 128], BF16)]:
            ext[nm] = nc.dram_tensor(nm, shape, dt, kind="ExternalInput").ap()
        for nm, shape, dt in [("qd", [4, 128, T], BF16), ("kd", [4, 128, T], BF16),
                              ("lad", [4, 128, T], F32), ("laRd", [4, 128, T], F32),
                              ("vtd", [NTl, 128, 512], BF16), ("tgd", [NTl, 128, 26], F32),
                              ("abx", [NTl, 128, 2], F32)]:
            ext[nm] = nc.dram_tensor(nm, shape, dt, kind="ExternalOutput").ap()
    x, wqT, wkT, waT, waRT, wvbgT = (ext[n] for n in
        ("x", "wqT", "wkT", "waT", "waRT", "wvbgT"))
    bav, baRv, bbgv, lnwv, lnbv, id16 = (ext[n] for n in
        ("bav", "baRv", "bbgv", "lnwv", "lnbv", "id16"))
    qd, kd, lad, laRd, vtd, tgd, abx = (ext[n] for n in
        ("qd", "kd", "lad", "laRd", "vtd", "tgd", "abx"))
    if debug:
        xnd = nc.dram_tensor("xnd", [8, 128, T], BF16, kind="ExternalOutput").ap()
        statd = nc.dram_tensor("statd", [T // SEG1, 128, 96], F32, kind="ExternalOutput").ap()
        statd1 = nc.dram_tensor("statd1", [T // SEG1, 128, 96], F32, kind="ExternalOutput").ap()
        statd2 = nc.dram_tensor("statd2", [T // SEG1, 128, 96], F32, kind="ExternalOutput").ap()
        xzd = nc.dram_tensor("xzd", [T // SEG1, 128, 1024], BF16, kind="ExternalOutput").ap()

    NSEG = T // SEG1
    TPS = SEG1 // 128

    with contextlib.ExitStack() as es:
        sb = lambda n, s, d: es.enter_context(nc.sbuf_tensor(pfx + n, s, d))
        ps = lambda n, s, d: es.enter_context(nc.psum_tensor(pfx + n, s, d))
        mksem = lambda n: Ctr(es.enter_context(nc.semaphore(pfx + n)))

        wq16 = sb("wq16", [128, 4096], BF16)
        wk16 = sb("wk16", [128, 4096], BF16)
        wa16 = sb("wa16", [128, 4096], BF16)
        waR16 = sb("waR16", [128, 4096], BF16)
        wvbg16 = sb("wvbg16", [128, 8 * 528], BF16)
        wld = sb("wld", [128, 4224], F32)
        bacol = sb("bacol", [128, 4], F32)
        baRcol = sb("baRcol", [128, 4], F32)
        bbgr = sb("bbgr", [1, 16], BF16)
        ones1 = sb("ones1", [1, 128], BF16)
        lnwc = sb("lnwc", [128, 8], F32)
        lnbc = sb("lnbc", [128, 8], F32)
        sid = sb("sid", [128, 128], BF16)
        ones2 = sb("ones2", [128, 2], BF16)

        xt = sb("xt", [128, SEG1 * 8], F32)
        xsq = sb("xsq", [128, 1024], F32)
        stat = sb("stat", [128, 96], F32)
        xz = sb("xz", [128, 1024], BF16)
        xn2f = sb("xn2f", [128, 8 * SEG1], BF16)
        qdo = sb("qdo", [128, 4 * SEG1], BF16)
        kdo = sb("kdo", [128, 4 * SEG1], BF16)
        lao = sb("lao", [128, 4 * SEG1], F32)
        laRo = sb("laRo", [128, 4 * SEG1], F32)
        vto = sb("vto", [128, TPS * 512], BF16)
        tgo = sb("tgo", [128, TPS * 26], F32)
        ksq = sb("ksq", [128, 512], BF16)
        epsc = sb("epsc", [128, 1], F32)
        n2r = sb("n2r", [32, 4 * SEG1], F32)
        trsc = sb("trsc", [128, 128], F32)
        rnr = sb("rnr", [32, 4 * SEG1], F32)

        pbf = ps("pbf", [128, 3072], F32)
        pb16 = ps("pb16", [128, 2048], BF16)

        sd = mksem("sd")
        st_ = mksem("st_")
        sv = mksem("sv")
        ss = mksem("ss")
        so = mksem("so")

        SUMX, SUMS, NEGMU, STC, S1C, A2C, B2C, SCR, TM1, TM2 = (i * 8 for i in range(10))
        bank = lambda i: pbf[:, 512 * i:512 * (i + 1)]

        wlist = [(wqT, wq16, 512), (wkT, wk16, 512), (waT, wa16, 512),
                 (waRT, waR16, 512), (wvbgT, wvbg16, 528)]
        for src, dst, ncol in wlist:
            with nc.Block() as blk:
                @blk.sync
                def _(s, src=src, ncol=ncol):
                    for kc in range(8):
                        sd.inc(s.dma_start(wld[:, kc * ncol:(kc + 1) * ncol],
                                           src[kc * 128:(kc + 1) * 128, :]), 16)
                    s.wait_ge(sd.h, sd.n)
            with nc.Block() as blk:
                @blk.vector
                def _(v, dst=dst, ncol=ncol):
                    v.tensor_copy(dst[:, 0:8 * ncol], wld[:, 0:8 * ncol])
        with nc.Block() as blk:
            @blk.sync
            def _(s):
                for oc in range(4):
                    sd.inc(s.dma_start(bacol[:, oc:oc + 1], bav[oc * 128:(oc + 1) * 128]), 16)
                    sd.inc(s.dma_start(baRcol[:, oc:oc + 1], baRv[oc * 128:(oc + 1) * 128]), 16)
                for dc in range(8):
                    sd.inc(s.dma_start(lnwc[:, dc:dc + 1], lnwv[dc * 128:(dc + 1) * 128]), 16)
                    sd.inc(s.dma_start(lnbc[:, dc:dc + 1], lnbv[dc * 128:(dc + 1) * 128]), 16)
                sd.inc(s.dma_start(sid[:, :], id16), 16)
                sd.inc(s.dma_start(wld[0:1, 0:16], bbgv), 16)
                s.wait_ge(sd.h, sd.n)
        with nc.Block() as blk:
            @blk.vector
            def _(v):
                v.tensor_copy(bbgr[:, :], wld[0:1, 0:16])
                v.memset(epsc[:, :], EPS)
                v.memset(ones1[:, :], 1.0)
                v.memset(ones2[:, :], 0.0)
                v.memset(ones2[0:64, 0:1], 1.0)
                v.memset(ones2[64:128, 1:2], 1.0)

        for seg in range(NSEG):
            t0 = seg * SEG1
            with nc.Block() as blk:
                @blk.sync
                def _(s):
                    for tt in range(TPS):
                        sd.inc(s.dma_start(xt[:, tt * 1024:(tt + 1) * 1024],
                                           x[t0 + tt * 128: t0 + (tt + 1) * 128, :]), 16)
                    s.wait_ge(sd.h, sd.n)

            with nc.Block() as blk:
                @blk.scalar
                def _(sc):
                    for tt in range(TPS):
                        sc.activation(xsq[:, :], xt[:, tt * 1024:(tt + 1) * 1024],
                                      AF.Square, accum_out=stat[:, SUMS + tt:SUMS + tt + 1])
                @blk.vector
                def _(v):
                    for tt in range(TPS):
                        v.reduce_sum(stat[:, SUMX + tt:SUMX + tt + 1],
                                     xt[:, tt * 1024:(tt + 1) * 1024], axis=AXX)

            A = lambda c: stat[:, c:c + 8]
            with nc.Block() as blk:
                @blk.vector
                def _(v):
                    def vw(inst):
                        inst.then_inc(sv.h, 1)
                        sv.n += 1
                        v.wait_ge(sv.h, sv.n)
                    v.tensor_scalar(A(NEGMU), A(SUMX), -1.0 / DM, None, ALU.mult)
                    vw(v.tensor_scalar(A(SCR), A(SUMS), 1.0 / DM, None, ALU.mult))
                    vw(v.tensor_tensor(A(S1C), A(NEGMU), A(NEGMU), ALU.mult))
                    vw(v.tensor_tensor(A(SCR), A(SCR), A(S1C), ALU.subtract))
                    vw(v.tensor_scalar(A(TM1), A(SCR), EPS, None, ALU.add))
                    vw(v.tensor_scalar(A(TM2), A(SCR), 1.0 + EPS, EPS * EPS,
                                       ALU.mult, ALU.add))
            if debug:
                with nc.Block() as blk:
                    @blk.sync
                    def _(s):
                        sd.inc(s.dma_start(statd1[seg, :, :], stat[:, :]), 16)
                        s.wait_ge(sd.h, sd.n)
            with nc.Block() as blk:
                @blk.scalar
                def _(sc):
                    sc.activation(A(S1C), A(TM1), AF.Ln).then_inc(ss.h, 1)
                    ss.n += 1
                    sc.wait_ge(ss.h, ss.n)
                    sc.activation(A(S1C), A(S1C), AF.Exp, scale=-0.5).then_inc(ss.h, 1)
                    ss.n += 1
                    sc.activation(A(STC), A(TM2), AF.Ln).then_inc(ss.h, 1)
                    ss.n += 1
                    sc.wait_ge(ss.h, ss.n)
                    sc.activation(A(STC), A(STC), AF.Exp, scale=-0.5).then_inc(ss.h, 1)
                    ss.n += 1
            if debug:
                with nc.Block() as blk:
                    @blk.sync
                    def _(s):
                        sd.inc(s.dma_start(statd2[seg, :, :], stat[:, :]), 16)
                        s.wait_ge(sd.h, sd.n)
            with nc.Block() as blk:
                @blk.vector
                def _(v):
                    def vw(inst):
                        inst.then_inc(sv.h, 1)
                        sv.n += 1
                        v.wait_ge(sv.h, sv.n)
                    for base, dcol in ((S1C, TM1), (STC, TM2)):
                        vw(v.tensor_tensor(A(SCR), A(base), A(base), ALU.mult))
                        vw(v.tensor_tensor(A(SCR), A(SCR), A(dcol), ALU.mult))
                        vw(v.tensor_scalar(A(SCR), A(SCR), -0.5, 1.5, ALU.mult, ALU.add))
                        vw(v.tensor_tensor(A(base), A(base), A(SCR), ALU.mult))
                    v.tensor_tensor(A(B2C), A(NEGMU), A(S1C), ALU.mult)
                    vw(v.tensor_scalar(A(A2C), A(S1C), 1.0, None, ALU.add))
                    for tt in range(TPS):
                        v.tensor_copy(tgo[:, tt * 26 + 24:tt * 26 + 25],
                                      stat[:, A2C + tt:A2C + tt + 1])
                        v.tensor_copy(tgo[:, tt * 26 + 25:tt * 26 + 26],
                                      stat[:, B2C + tt:B2C + tt + 1])

            v0, t0n = sv.n, st_.n
            with nc.Block() as blk:
                @blk.vector
                def _(v):
                    for tt in range(TPS):
                        if tt > 0:
                            v.wait_ge(st_.h, t0n + 8 * tt)
                        v.tensor_scalar(xz[:, :], xt[:, tt * 1024:(tt + 1) * 1024],
                                        stat[:, NEGMU + tt:NEGMU + tt + 1],
                                        stat[:, STC + tt:STC + tt + 1],
                                        ALU.add, ALU.mult).then_inc(sv.h, 1)
                        v.wait_ge(st_.h, t0n + 8 * (tt + 1))
                        for dc in range(8):
                            v.tensor_scalar(
                                xn2f[:, dc * SEG1 + tt * 128: dc * SEG1 + (tt + 1) * 128],
                                pb16[:, dc * 128:(dc + 1) * 128],
                                lnwc[:, dc:dc + 1], lnbc[:, dc:dc + 1],
                                ALU.mult, ALU.add).then_inc(sv.h, 1)
                @blk.tensor
                def _(t):
                    for tt in range(TPS):
                        t.wait_ge(sv.h, v0 + 9 * tt + 1)
                        for dc in range(8):
                            t.transpose(pb16[:, dc * 128:(dc + 1) * 128],
                                        xz[:, dc * 128:(dc + 1) * 128],
                                        sid[:, :]).then_inc(st_.h, 1)
            sv.n = v0 + 9 * TPS
            st_.n = t0n + 8 * TPS

            wsrc = [wq16, wk16, wa16, waR16]
            groups = [(ti, oc) for ti in range(4) for oc in range(4)]
            tb, vb, sb_ = st_.n, sv.n, ss.n
            if not (stages & 1):
                groups = []
            with nc.Block() as blk:
                @blk.tensor
                def _(t):
                    for mt in range(2):
                        for gi, (ti, oc) in enumerate(groups):
                            gidx = mt * 16 + gi
                            bk = gidx % 6
                            if gidx >= 6:
                                j = gidx - 6
                                jmt, jgi = divmod(j, 16)
                                jti = groups[jgi][0]
                                nvec = sum(1 for m2 in range(2) for g2, (t2, _) in enumerate(groups)
                                           if m2 * 16 + g2 <= j and t2 < 2)
                                nsc = (j + 1) - nvec
                                if jti < 2:
                                    t.wait_ge(sv.h, vb + nvec)
                                else:
                                    t.wait_ge(ss.h, sb_ + nsc)
                            mm = None
                            for kc in range(8):
                                mm = t.matmul(
                                    bank(bk)[:, 0:512],
                                    wsrc[ti][:, kc * 512 + oc * 128: kc * 512 + (oc + 1) * 128],
                                    xn2f[:, kc * SEG1 + mt * 512: kc * SEG1 + mt * 512 + 512],
                                    start=(kc == 0), stop=(kc == 7))
                            mm.then_inc(st_.h, 1)
                @blk.vector
                def _(v):
                    for mt in range(2):
                        for gi, (ti, oc) in enumerate(groups):
                            if ti >= 2:
                                continue
                            gidx = mt * 16 + gi
                            bk = gidx % 6
                            v.wait_ge(st_.h, tb + gidx + 1)
                            co = oc * SEG1 + mt * 512
                            dst = qdo if ti == 0 else kdo
                            v.tensor_copy(dst[:, co:co + 512], bank(bk)[:, 0:512]
                                          ).then_inc(sv.h, 1)
                @blk.scalar
                def _(sc):
                    for mt in range(2):
                        for gi, (ti, oc) in enumerate(groups):
                            if ti < 2:
                                continue
                            gidx = mt * 16 + gi
                            bk = gidx % 6
                            sc.wait_ge(st_.h, tb + gidx + 1)
                            co = oc * SEG1 + mt * 512
                            bc = bacol if ti == 2 else baRcol
                            dst = lao if ti == 2 else laRo
                            sc.activation(dst[:, co:co + 512], bank(bk)[:, 0:512],
                                          AF.Sigmoid, bias=bc[:, oc:oc + 1]
                                          ).then_inc(ss.h, 1)
            st_.n = tb + 2 * len(groups)
            sv.n = vb + len(groups)
            ss.n = sb_ + len(groups)

            tb, vb, sb_ = st_.n, sv.n, ss.n
            NRM = 8 if (stages & 2) else 0
            with nc.Block() as blk:
                @blk.scalar
                def _(sc):
                    for i in range(NRM):
                        oc, mt = divmod(i, 2)
                        if i > 0:
                            sc.wait_ge(st_.h, tb + i)
                        co = oc * SEG1 + mt * 512
                        sc.activation(ksq[:, :], kdo[:, co:co + 512], AF.Square
                                      ).then_inc(ss.h, 1)
                @blk.tensor
                def _(t):
                    for i in range(NRM):
                        t.wait_ge(ss.h, sb_ + i + 1)
                        if i > 0:
                            t.wait_ge(sv.h, vb + i)
                        t.matmul(pbf[0:2, 2560:3072], ones2[:, :], ksq[:, :],
                                 start=True, stop=True).then_inc(st_.h, 1)
                @blk.vector
                def _(v):
                    for i in range(NRM):
                        oc, mt = divmod(i, 2)
                        v.wait_ge(st_.h, tb + i + 1)
                        v.tensor_copy(n2r[0:2, oc * SEG1 + mt * 512: oc * SEG1 + mt * 512 + 512],
                                      pbf[0:2, 2560:3072]).then_inc(sv.h, 1)
            st_.n = tb + NRM
            sv.n = vb + NRM
            ss.n = sb_ + NRM

            with nc.Block() as blk:
                @blk.vector
                def _(v):
                    v.tensor_scalar(n2r[0:2, 0:SEG1 * 4], n2r[0:2, 0:SEG1 * 4],
                                    1e-24, None, ALU.max)
            with nc.Block() as blk:
                @blk.scalar
                def _(sc):
                    if not (stages & 8):
                        return
                    sc.activation(lao[:, :], lao[:, :], AF.Ln)
                    sc.activation(laRo[:, :], laRo[:, :], AF.Ln)
                    sc.activation(rnr[0:2, :], n2r[0:2, :], AF.Ln).then_inc(ss.h, 1)
                    ss.n += 1
                    sc.wait_ge(ss.h, ss.n)
                    sc.activation(rnr[0:2, :], rnr[0:2, :],
                                  AF.Exp, scale=-0.5).then_inc(ss.h, 1)
                    ss.n += 1
            with nc.Block() as blk:
                @blk.vector
                def _(v):
                    def vw(inst):
                        inst.then_inc(sv.h, 1)
                        sv.n += 1
                        v.wait_ge(sv.h, sv.n)
                    for ch in range(4):
                        rna = rnr[0:2, ch * SEG1:(ch + 1) * SEG1]
                        n2c = n2r[0:2, ch * SEG1:(ch + 1) * SEG1]
                        tmp = xsq[0:2, 0:SEG1]
                        vw(v.tensor_tensor(tmp, rna, rna, ALU.mult))
                        vw(v.tensor_tensor(tmp, tmp, n2c, ALU.mult))
                        vw(v.tensor_scalar(tmp, tmp, -0.5, 1.5, ALU.mult, ALU.add))
                        vw(v.tensor_tensor(rna, rna, tmp, ALU.mult))

            with nc.Block() as blk:
                @blk.vector
                def _(v):
                    for tt in range(TPS):
                        last = None
                        for oc in range(4):
                            for bi in range(4):
                                last = v.transpose(
                                    trsc[32 * bi:32 * (bi + 1), oc * 32:oc * 32 + 32],
                                    rnr[0:32, oc * SEG1 + tt * 128 + 32 * bi:
                                        oc * SEG1 + tt * 128 + 32 * (bi + 1)])
                        last.then_inc(sv.h, 1)
                        sv.n += 1
                        v.wait_ge(sv.h, sv.n)
                        v.tensor_copy(
                            tgo[:, tt * 26 + 16: tt * 26 + 24].rearrange("p (a b) -> p a b", a=4),
                            trsc[:, :].rearrange("p (a b) -> p a b", a=4)[:, :, 0:2])

            tb, vb, sb_ = st_.n, sv.n, ss.n
            with nc.Block() as blk:
                @blk.tensor
                def _(t):
                    for tt in range(TPS if (stages & 4) else 0):
                        bkv = tt % 2
                        bkg = 2 + tt % 2
                        if tt >= 2:
                            t.wait_ge(sv.h, vb + tt - 2 + 1)
                            t.wait_ge(ss.h, sb_ + tt - 2 + 1)
                        for kc in range(8):
                            t.matmul(bank(bkv)[:, 0:512],
                                     xn2f[:, kc * SEG1 + tt * 128: kc * SEG1 + (tt + 1) * 128],
                                     wvbg16[:, kc * 528: kc * 528 + 512],
                                     start=(kc == 0), stop=(kc == 7))
                            t.matmul(bank(bkg)[:, 0:16],
                                     xn2f[:, kc * SEG1 + tt * 128: kc * SEG1 + (tt + 1) * 128],
                                     wvbg16[:, kc * 528 + 512: kc * 528 + 528],
                                     start=(kc == 0), stop=False)
                        t.matmul(bank(bkg)[:, 0:16], ones1[:, :], bbgr[:, :],
                                 start=False, stop=True).then_inc(st_.h, 1)
                @blk.vector
                def _(v):
                    for tt in range(TPS if (stages & 4) else 0):
                        bkv = tt % 2
                        v.wait_ge(st_.h, tb + tt + 1)
                        v.tensor_copy(vto[:, tt * 512:(tt + 1) * 512],
                                      bank(bkv)[:, 0:512]).then_inc(sv.h, 1)
                @blk.scalar
                def _(sc):
                    for tt in range(TPS if (stages & 4) else 0):
                        bkg = 2 + tt % 2
                        sc.wait_ge(st_.h, tb + tt + 1)
                        sc.activation(tgo[:, tt * 26:tt * 26 + 16], bank(bkg)[:, 0:16],
                                      AF.Sigmoid).then_inc(ss.h, 1)
            if stages & 4:
                st_.n = tb + TPS
                sv.n = vb + TPS
                ss.n = sb_ + TPS

            with nc.Block() as blk:
                @blk.sync
                def _(s):
                    for oc in range(4):
                        so.inc(s.dma_start(qd[oc, :, t0:t0 + SEG1],
                                           qdo[:, oc * SEG1:(oc + 1) * SEG1]), 16)
                        so.inc(s.dma_start(kd[oc, :, t0:t0 + SEG1],
                                           kdo[:, oc * SEG1:(oc + 1) * SEG1]), 16)
                        so.inc(s.dma_start(lad[oc, :, t0:t0 + SEG1],
                                           lao[:, oc * SEG1:(oc + 1) * SEG1]), 16)
                        so.inc(s.dma_start(laRd[oc, :, t0:t0 + SEG1],
                                           laRo[:, oc * SEG1:(oc + 1) * SEG1]), 16)
                    if debug:
                        so.inc(s.dma_start(statd[seg, :, :], stat[:, :]), 16)
                        so.inc(s.dma_start(xzd[seg, :, :], xz[:, :]), 16)
                        for dc in range(8):
                            so.inc(s.dma_start(xnd[dc, :, t0:t0 + SEG1],
                                               xn2f[:, dc * SEG1:(dc + 1) * SEG1]), 16)
                    for tt in range(TPS):
                        so.inc(s.dma_start(vtd[seg * TPS + tt, :, :],
                                           vto[:, tt * 512:(tt + 1) * 512]), 16)
                        so.inc(s.dma_start(tgd[seg * TPS + tt, :, :],
                                           tgo[:, tt * 26:(tt + 1) * 26]), 16)
                        so.inc(s.dma_start(abx[seg * TPS + tt, :, :],
                                           tgo[:, tt * 26 + 24:(tt + 1) * 26]), 16)
                    s.wait_ge(so.h, so.n)
    return nc


def _hostprep_phase1(inputs, core):
    b, hh = core // 2, core % 2
    hs = slice(512 * hh, 512 * hh + 512)
    hs8 = slice(8 * hh, 8 * hh + 8)
    f = np.float32
    cc = np.ascontiguousarray
    wvbgT = np.concatenate([
        np.asarray(inputs["L0_attn_Wv"], f)[hs].T,
        np.asarray(inputs["L0_attn_Wb"], f)[hs8].T,
        np.asarray(inputs["L0_attn_Wg"], f)[hs8].T], axis=1)
    return {
        "x": cc(np.asarray(inputs["L0_attn_x"], f)[b]),
        "wqT": cc(np.asarray(inputs["L0_attn_Wq"], f)[hs].T),
        "wkT": cc(np.asarray(inputs["L0_attn_Wk"], f)[hs].T),
        "waT": cc(np.asarray(inputs["L0_attn_Wa"], f)[hs].T),
        "waRT": cc(np.asarray(inputs["L0_attn_WaR"], f)[hs].T),
        "wvbgT": cc(wvbgT),
        "bav": cc(np.asarray(inputs["L0_attn_ba"], f)[hs]),
        "baRv": cc(np.asarray(inputs["L0_attn_baR"], f)[hs]),
        "bbgv": cc(np.concatenate([np.asarray(inputs["L0_attn_bb"], f)[hs8],
                                   np.asarray(inputs["L0_attn_bg"], f)[hs8]])),
        "lnwv": cc(np.asarray(inputs["L0_attn_ln_w"], f)),
        "lnbv": cc(np.asarray(inputs["L0_attn_ln_b"], f)),
        "id16": np.eye(128, dtype=BF).view(np.uint16),
    }


def kernel(**inputs):
    raise NotImplementedError("phases under construction")


def build_phase2(nc, stages2=255, ext=None, pfx=""):
    if ext is None:
        ext = {}
        for nm, shape, dt in [("qd", [4, 128, T], BF16), ("kd", [4, 128, T], BF16),
                              ("lad", [4, 128, T], F32), ("laRd", [4, 128, T], F32),
                              ("vtd", [T // 128, 128, 512], BF16),
                              ("tgd", [T // 128, 128, 26], F32),
                              ("id16", [128, 128], BF16), ("mks", [128, 3 * 128], BF16)]:
            ext[nm] = nc.dram_tensor(nm, shape, dt, kind="ExternalInput").ap()
        ext["ofd"] = nc.dram_tensor("ofd", [8, 128, T // 2], BF16, kind="ExternalOutput").ap()
    qd, kd, lad, laRd, vtd, tgd, id16, mks, ofd = (ext[n] for n in
        ("qd", "kd", "lad", "laRd", "vtd", "tgd", "id16", "mks", "ofd"))

    NSEG = T // 512
    NCHl = T // C
    with contextlib.ExitStack() as es:
        sb = lambda n, s, d: es.enter_context(nc.sbuf_tensor(pfx + n, s, d))
        ps = lambda n, s, d: es.enter_context(nc.psum_tensor(pfx + n, s, d))
        mksem = lambda n: Ctr(es.enter_context(nc.semaphore(pfx + n)))

        sid = sb("sid", [128, 128], BF16)
        mLS = sb("mLS", [128, 128], BF16)
        mUS = sb("mUS", [128, 128], BF16)
        mUI = sb("mUI", [128, 128], BF16)
        i16 = sb("i16", [128, 128], BF16)

        qds = sb("qds", [128, 2 * 2048], BF16)
        kds = sb("kds", [128, 2 * 2048], BF16)
        las = sb("las", [128, 2 * 2048], F32)
        laRs = sb("laRs", [128, 2 * 2048], F32)
        vts = sb("vts", [128, 2 * 2048], BF16)
        tgs = sb("tgs", [128, 2 * 104], F32)
        ofo = sb("ofo", [128, 2 * 2048], BF16)

        labA = [sb(f"labA{p}", [128, 129], F32) for p in range(4)]
        labR = [sb(f"labR{p}", [128, 129], F32) for p in range(4)]
        ncpA = sb("ncpA", [128, 4], F32)
        pcpA = sb("pcpA", [128, 4], F32)
        ncpR = sb("ncpR", [128, 4], F32)
        pcpR = sb("pcpR", [128, 4], F32)
        eclA = sb("eclA", [128, 4], F32)
        eclR = sb("eclR", [128, 4], F32)
        EpA = [sb(f"EpA{p}", [128, 128], F32) for p in range(4)]
        EmA = [sb(f"EmA{p}", [128, 128], F32) for p in range(4)]
        EpP = [sb(f"EpP{p}", [128, 128], F32) for p in range(4)]
        EpR = [sb(f"EpR{p}", [128, 128], F32) for p in range(4)]
        EmR = [sb(f"EmR{p}", [128, 128], F32) for p in range(4)]
        kA = sb("kA", [128, 4 * 128], BF16)
        kIA = sb("kIA", [128, 4 * 128], BF16)
        kPRE = sb("kPRE", [128, 4 * 128], BF16)
        qA = sb("qA", [128, 4 * 128], BF16)
        kAR = sb("kAR", [128, 4 * 128], BF16)
        kIAR = sb("kIAR", [128, 4 * 128], BF16)
        qAR = sb("qAR", [128, 4 * 128], BF16)
        kiat = sb("kiat", [128, 4 * 128], BF16)
        kiart = sb("kiart", [128, 4 * 128], BF16)
        vsb = sb("vsb", [128, 512], BF16)
        scl = sb("scl", [128, 64], F32)
        BRN, NBRN2, GRN, NGRN2, NRN, SC5, SC6, SC7 = (i * 8 for i in range(8))

        nbt = sb("nbt", [128, 16 * 128], BF16)
        nbtT = sb("nbtT", [128, 16 * 128], BF16)
        pwA = sb("pwA", [128, 16 * 128], BF16)
        pwAT = sb("pwAT", [128, 16 * 128], BF16)
        pwB = sb("pwB", [128, 16 * 128], BF16)
        pwBT = sb("pwBT", [128, 16 * 128], BF16)
        Tc = sb("Tc", [128, 16 * 128], BF16)
        gq = sb("gq", [128, 8 * 128], BF16)
        gqR = sb("gqR", [128, 8 * 128], BF16)
        gpre = sb("gpre", [128, 8 * 128], BF16)
        xs = sb("xs", [128, 8 * 64], BF16)
        ub = sb("ub", [128, 8 * 64], BF16)
        ubR = sb("ubR", [128, 8 * 64], BF16)
        rs = sb("rs", [128, 8 * 64], BF16)
        rtmp = sb("rtmp", [128, 8 * 64], F32)
        oT = sb("oT", [128, 8 * 64], BF16)
        spr = sb("spr", [128, 4 * 128], BF16)

        pbf = ps("pbf", [128, 3072], F32)
        pb16 = ps("pb16", [128, 2048], BF16)
        bank = lambda i: pbf[:, 512 * i:512 * (i + 1)]

        sd = mksem("sd")
        st_ = mksem("st_")
        sv = mksem("sv")
        ss = mksem("ss")
        so = mksem("so")

        def seg_load(s, seg, buf):
            t0 = seg * 512
            for oc in range(4):
                sd.inc(s.dma_start(qds[:, buf * 2048 + oc * 512: buf * 2048 + (oc + 1) * 512],
                                   qd[oc, :, t0:t0 + 512]), 16)
                sd.inc(s.dma_start(kds[:, buf * 2048 + oc * 512: buf * 2048 + (oc + 1) * 512],
                                   kd[oc, :, t0:t0 + 512]), 16)
                sd.inc(s.dma_start(las[:, buf * 2048 + oc * 512: buf * 2048 + (oc + 1) * 512],
                                   lad[oc, :, t0:t0 + 512]), 16)
                sd.inc(s.dma_start(laRs[:, buf * 2048 + oc * 512: buf * 2048 + (oc + 1) * 512],
                                   laRd[oc, :, t0:t0 + 512]), 16)
            for tt in range(4):
                sd.inc(s.dma_start(vts[:, buf * 2048 + tt * 512: buf * 2048 + (tt + 1) * 512],
                                   vtd[seg * 4 + tt, :, :]), 16)
                sd.inc(s.dma_start(tgs[:, buf * 104 + tt * 26: buf * 104 + (tt + 1) * 26],
                                   tgd[seg * 4 + tt, :, :]), 16)
            return sd.n

        with nc.Block() as blk:
            @blk.sync
            def _(s):
                sd.inc(s.dma_start(sid[:, :], id16), 16)
                sd.inc(s.dma_start(mLS[:, :], mks[:, 0:128]), 16)
                sd.inc(s.dma_start(mUS[:, :], mks[:, 128:256]), 16)
                sd.inc(s.dma_start(mUI[:, :], mks[:, 256:384]), 16)
                seg_load(s, 0, 0)
                s.wait_ge(sd.h, sd.n)
            @blk.vector
            def _(v):
                v.memset(spr[:, :], 0.0)
                for p in range(4):
                    v.memset(labA[p][:, 0:1], 0.0)
                    v.memset(labR[p][:, 0:1], 0.0)
                v.tensor_copy(i16[:, :], i16[:, :])
        with nc.Block() as blk:
            @blk.vector
            def _(v):
                v.tensor_copy(i16[:, :], sid[:, :])

        segload_done = [sd.n]

        for ci in range(NCHl):
            seg, cis = divmod(ci, 4)
            buf = seg % 2
            tgc = lambda c0, n=8: tgs[:, buf * 104 + cis * 26 + c0: buf * 104 + cis * 26 + c0 + n]
            qsl = lambda p: qds[:, buf * 2048 + p * 512 + cis * 128: buf * 2048 + p * 512 + (cis + 1) * 128]
            ksl = lambda p: kds[:, buf * 2048 + p * 512 + cis * 128: buf * 2048 + p * 512 + (cis + 1) * 128]
            lasl = lambda p: las[:, buf * 2048 + p * 512 + cis * 128: buf * 2048 + p * 512 + (cis + 1) * 128]
            laRsl = lambda p: laRs[:, buf * 2048 + p * 512 + cis * 128: buf * 2048 + p * 512 + (cis + 1) * 128]
            vsl = lambda h: vts[:, buf * 2048 + cis * 512 + h * 64: buf * 2048 + cis * 512 + (h + 1) * 64]
            ofsl = ofo[:, (seg % 2) * 2048:]

            if cis == 0:
                if seg + 1 < NSEG:
                    with nc.Block() as blk:
                        @blk.scalar
                        def _(s):
                            s.wait_ge(so.h, so.n)
                            seg_load(s, seg + 1, (seg + 1) % 2)
                    segload_done.append(sd.n)
                with nc.Block() as blk:
                    @blk.sync
                    def _(s):
                        s.wait_ge(sd.h, segload_done[seg])

            tb, vb, sb0 = st_.n, sv.n, ss.n
            if not (stages2 & 1):
                continue
            with nc.Block() as blk:
                @blk.vector
                def _(v):
                    def vw(inst):
                        inst.then_inc(sv.h, 1); sv.n += 1; v.wait_ge(sv.h, sv.n)
                    v.tensor_tensor(scl[:, BRN:BRN + 8], tgc(0), tgc(16), ALU.mult)
                    v.tensor_tensor(scl[:, GRN:GRN + 8], tgc(8), tgc(16), ALU.mult)
                    vw(v.tensor_scalar(scl[:, NRN:NRN + 8], tgc(16), -1.0, None, ALU.mult))
                    v.tensor_tensor(scl[:, NBRN2:NBRN2 + 8], scl[:, BRN:BRN + 8], scl[:, NRN:NRN + 8], ALU.mult)
                    vw(v.tensor_tensor(scl[:, NGRN2:NGRN2 + 8], scl[:, GRN:GRN + 8], scl[:, NRN:NRN + 8], ALU.mult))
                    last = None
                    for p in range(4):
                        last = v.tensor_tensor_scan(labA[p][:, 1:129], lasl(p), lasl(p), 0.0,
                                                    ALU.add, ALU.bypass)
                        last = v.tensor_tensor_scan(labR[p][:, 1:129], laRsl(p), laRsl(p), 0.0,
                                                    ALU.add, ALU.bypass)
                    last.then_inc(sv.h, 1); sv.n += 1
                    v.wait_ge(sv.h, sv.n)
                    for p in range(4):
                        v.tensor_scalar(ncpA[:, p:p + 1], labA[p][:, 128:129], -0.5, None, ALU.mult)
                        v.tensor_scalar(pcpA[:, p:p + 1], labA[p][:, 128:129], 0.5, None, ALU.mult)
                        v.tensor_scalar(ncpR[:, p:p + 1], labR[p][:, 128:129], -0.5, None, ALU.mult)
                        last = v.tensor_scalar(pcpR[:, p:p + 1], labR[p][:, 128:129], 0.5, None, ALU.mult)
                    last.then_inc(sv.h, 1); sv.n += 1
                @blk.scalar
                def _(sc):
                    sc.wait_ge(sv.h, vb + 4)
                    last = None
                    for p in range(4):
                        sc.activation(EpA[p][:, :], labA[p][:, 1:129], AF.Exp, bias=ncpA[:, p:p + 1])
                        sc.activation(EmA[p][:, :], labA[p][:, 1:129], AF.Exp, scale=-1.0, bias=pcpA[:, p:p + 1])
                        sc.activation(EpP[p][:, :], labA[p][:, 0:128], AF.Exp, bias=ncpA[:, p:p + 1])
                        sc.activation(EpR[p][:, :], labR[p][:, 1:129], AF.Exp, bias=ncpR[:, p:p + 1])
                        sc.activation(EmR[p][:, :], labR[p][:, 1:129], AF.Exp, scale=-1.0, bias=pcpR[:, p:p + 1])
                        sc.activation(eclA[:, p:p + 1], labA[p][:, 128:129], AF.Exp, scale=0.5)
                        last = sc.activation(eclR[:, p:p + 1], labR[p][:, 128:129], AF.Exp, scale=0.5)
                    last.then_inc(ss.h, 1); ss.n += 1
            tb, vb, sb0 = st_.n, sv.n, ss.n
            with nc.Block() as blk:
                @blk.vector
                def _(v):
                    v.wait_ge(ss.h, sb0)
                    last = None
                    for p in range(4):
                        sl = slice(p * 128, (p + 1) * 128)
                        v.tensor_tensor(kA[:, sl], ksl(p), EpA[p][:, :], ALU.mult)
                        v.tensor_tensor(kIA[:, sl], ksl(p), EmA[p][:, :], ALU.mult)
                        v.tensor_tensor(kPRE[:, sl], ksl(p), EpP[p][:, :], ALU.mult)
                        v.tensor_tensor(qA[:, sl], qsl(p), EpA[p][:, :], ALU.mult)
                        v.tensor_tensor(kAR[:, sl], ksl(p), EpR[p][:, :], ALU.mult)
                        v.tensor_tensor(kIAR[:, sl], ksl(p), EmR[p][:, :], ALU.mult)
                        last = v.tensor_tensor(qAR[:, sl], qsl(p), EpR[p][:, :], ALU.mult)
                    for h in range(8):
                        last = v.tensor_scalar(vsb[:, h * 64:(h + 1) * 64], vsl(h),
                                               scl[:, BRN + h:BRN + h + 1], None, ALU.mult)
                    for p in range(4):
                        v.tensor_scalar(spr[:, p * 128:p * 128 + 64], spr[:, p * 128:p * 128 + 64],
                                        eclA[:, p:p + 1], None, ALU.mult)
                        last = v.tensor_scalar(spr[:, p * 128 + 64:p * 128 + 128],
                                               spr[:, p * 128 + 64:p * 128 + 128],
                                               eclR[:, p:p + 1], None, ALU.mult)
                    last.then_inc(sv.h, 1); sv.n += 1
                @blk.tensor
                def _(t):
                    t.wait_ge(sv.h, sv.n)
                    for p in range(4):
                        t.transpose(pb16[:, p * 128:(p + 1) * 128], kIA[:, p * 128:(p + 1) * 128],
                                    sid[:, :]).then_inc(st_.h, 1)
                        t.transpose(pb16[:, 512 + p * 128:512 + (p + 1) * 128],
                                    kIAR[:, p * 128:(p + 1) * 128], sid[:, :]).then_inc(st_.h, 1)
                    st_.n += 8
                @blk.scalar
                def _(sc):
                    sc.wait_ge(st_.h, tb + 8)
                    for p in range(4):
                        sc.copy(kiat[:, p * 128:(p + 1) * 128],
                                pb16[:, p * 128:(p + 1) * 128]).then_inc(ss.h, 1)
                        sc.copy(kiart[:, p * 128:(p + 1) * 128],
                                pb16[:, 512 + p * 128:512 + (p + 1) * 128]).then_inc(ss.h, 1)
                        ss.n += 2

            if not (stages2 & 2):
                continue
            units = [(h, s_) for h in range(8) for s_ in range(2)]
            gslot = {}
            evn = [ui for ui, (h, s_) in enumerate(units) if h % 2 == 0]
            odd = [ui for ui, (h, s_) in enumerate(units) if h % 2 == 1]
            for j, ui in enumerate(evn):
                gslot[ui] = (j // 4, j % 4)
            for j, ui in enumerate(odd):
                gslot[ui] = (2 + j // 4, j % 4)
            tb, vb, sb0 = st_.n, sv.n, ss.n
            curs = [(pwA, pwAT), (pwB, pwBT)]
            tbl = lambda lv: tb + 32 + (lv - 1) * 48
            sbl = lambda lv: sb0 + 16 + (lv - 1) * 32
            with nc.Block() as blk:
                @blk.tensor
                def _(t):
                    for ui, (h, s_) in enumerate(units):
                        p, hp = h // 2, 64 * (h % 2)
                        ka_ = (kA if s_ == 0 else kAR)[hp:hp + 64, p * 128:(p + 1) * 128]
                        ki_ = (kIA if s_ == 0 else kIAR)[hp:hp + 64, p * 128:(p + 1) * 128]
                        bk, slot = gslot[ui]
                        t.matmul(pbf[:, bk * 512 + slot * 128: bk * 512 + (slot + 1) * 128],
                                 ka_, ki_, start=True, stop=True).then_inc(st_.h, 1)
                    t.wait_ge(sv.h, vb + 16)
                    for w in range(2):
                        if w > 0:
                            t.wait_ge(ss.h, sb0 + 8)
                        for k in range(8):
                            ui = w * 8 + k
                            t.transpose(pb16[:, k * 128:(k + 1) * 128],
                                        nbt[:, ui * 128:(ui + 1) * 128], sid[:, :]
                                        ).then_inc(st_.h, 1)
                    prev, prevT = nbt, nbtT
                    for lv in range(1, 7):
                        cur, curT = curs[(lv - 1) % 2]
                        t.wait_ge(ss.h, sbl(lv))
                        for ui in range(16):
                            bk, slot = divmod(ui, 4)
                            t.matmul(pbf[:, bk * 512 + slot * 128: bk * 512 + (slot + 1) * 128],
                                     prevT[:, ui * 128:(ui + 1) * 128],
                                     prev[:, ui * 128:(ui + 1) * 128],
                                     start=True, stop=True).then_inc(st_.h, 1)
                        if lv < 6:
                            t.wait_ge(sv.h, vb + 32 + (lv - 1) * 16)
                            for ui in range(16):
                                if ui < 8:
                                    bk, slot = 4 + ui // 4, ui % 4
                                else:
                                    bk, slot = (ui - 8) // 4, (ui - 8) % 4
                                    t.wait_ge(ss.h, sbl(lv) + 8 + (4 if ui >= 12 else 0))
                                t.matmul(pbf[:, bk * 512 + slot * 128: bk * 512 + (slot + 1) * 128],
                                         prev[:, ui * 128:(ui + 1) * 128],
                                         prevT[:, ui * 128:(ui + 1) * 128],
                                         start=True, stop=True).then_inc(st_.h, 1)
                        t.wait_ge(ss.h, sbl(lv) + (16 if lv == 6 else 32))
                        t.wait_ge(sv.h, vb + 32 + (lv - 1) * 16)
                        for w in range(2):
                            if w > 0:
                                t.wait_ge(sv.h, vb + 32 + (lv - 1) * 16 + 8)
                            for k in range(8):
                                ui = w * 8 + k
                                bk, slot = 4 + k // 4, k % 4
                                t.matmul(pbf[:, bk * 512 + slot * 128: bk * 512 + (slot + 1) * 128],
                                         cur[:, ui * 128:(ui + 1) * 128],
                                         Tc[:, ui * 128:(ui + 1) * 128],
                                         start=True, stop=True).then_inc(st_.h, 1)
                        prev, prevT = cur, curT
                    t.wait_ge(ss.h, sbl(6) + 16)
                    heads_half = [[0, 2, 4, 6], [1, 3, 5, 7]]
                    for half in range(2):
                        if half == 1:
                            t.wait_ge(sv.h, vb + 128 + 12)
                        k = 0
                        for h in heads_half[half]:
                            p, hp = h // 2, 64 * (h % 2)
                            for (gdst, lh, rh) in ((gq, kIA, qA), (gqR, kIAR, qAR), (gpre, kIA, kPRE)):
                                bk, slot = divmod(k, 4)
                                t.matmul(pbf[:, bk * 512 + slot * 128: bk * 512 + (slot + 1) * 128],
                                         lh[hp:hp + 64, p * 128:(p + 1) * 128],
                                         rh[hp:hp + 64, p * 128:(p + 1) * 128],
                                         start=True, stop=True).then_inc(st_.h, 1)
                                k += 1
                    st_.n = tb + 328
                @blk.scalar
                def _(sc):
                    for w in range(2):
                        sc.wait_ge(st_.h, tb + 16 + 8 * (w + 1))
                        for k in range(8):
                            ui = w * 8 + k
                            sc.copy(nbtT[:, ui * 128:(ui + 1) * 128],
                                    pb16[:, k * 128:(k + 1) * 128]).then_inc(ss.h, 1)
                    for lv in range(1, 7):
                        cur, curT = curs[(lv - 1) % 2]
                        for bk in range(4):
                            sc.wait_ge(st_.h, tbl(lv) + (bk + 1) * 4)
                            sc.copy(cur[:, bk * 512:(bk + 1) * 512],
                                    pbf[:, bk * 512:(bk + 1) * 512]).then_inc(ss.h, 4)
                        if lv < 6:
                            for j in range(2):
                                sc.wait_ge(st_.h, tbl(lv) + 16 + (j + 1) * 4)
                                sc.copy(curT[:, j * 512:(j + 1) * 512],
                                        pbf[:, (4 + j) * 512:(5 + j) * 512]).then_inc(ss.h, 4)
                            for j in range(2):
                                sc.wait_ge(st_.h, tbl(lv) + 24 + (j + 1) * 4)
                                sc.copy(curT[:, (2 + j) * 512:(3 + j) * 512],
                                        pbf[:, j * 512:(j + 1) * 512]).then_inc(ss.h, 4)
                    ss.n = sb0 + 192
                @blk.vector
                def _(v):
                    for ui, (h, s_) in enumerate(units):
                        bk, slot = gslot[ui]
                        v.wait_ge(st_.h, tb + 16)
                        sccol = scl[:, (NBRN2 if s_ == 0 else NGRN2) + h:
                                    (NBRN2 if s_ == 0 else NGRN2) + h + 1]
                        v.scalar_tensor_tensor(nbt[:, ui * 128:(ui + 1) * 128],
                                               pbf[:, bk * 512 + slot * 128: bk * 512 + (slot + 1) * 128],
                                               sccol, mLS[:, :], ALU.mult, ALU.mult
                                               ).then_inc(sv.h, 1)
                    for ui in range(16):
                        v.wait_ge(ss.h, sb0 + ui + 1)
                        v.tensor_tensor(Tc[:, ui * 128:(ui + 1) * 128],
                                        nbtT[:, ui * 128:(ui + 1) * 128], i16[:, :], ALU.add
                                        ).then_inc(sv.h, 1)
                    for lv in range(1, 7):
                        sqc = 32 if lv < 6 else 16
                        for w in range(2):
                            v.wait_ge(st_.h, tbl(lv) + sqc + 8 * (w + 1))
                            for k in range(8):
                                ui = w * 8 + k
                                bk, slot = 4 + k // 4, k % 4
                                v.tensor_tensor(Tc[:, ui * 128:(ui + 1) * 128],
                                                pbf[:, bk * 512 + slot * 128: bk * 512 + (slot + 1) * 128],
                                                Tc[:, ui * 128:(ui + 1) * 128], ALU.add
                                                ).then_inc(sv.h, 1)
                    heads_half = [[0, 2, 4, 6], [1, 3, 5, 7]]
                    for half in range(2):
                        v.wait_ge(st_.h, tb + 304 + 12 * (half + 1))
                        k = 0
                        for h in heads_half[half]:
                            for (gdst, msk) in ((gq, mUI), (gqR, mUI), (gpre, mUS)):
                                bk, slot = divmod(k, 4)
                                v.tensor_tensor(gdst[:, h * 128:(h + 1) * 128],
                                                pbf[:, bk * 512 + slot * 128: bk * 512 + (slot + 1) * 128],
                                                msk[:, :], ALU.mult).then_inc(sv.h, 1)
                                k += 1
                    sv.n = vb + 152

            if not (stages2 & 4):
                continue
            tb, vb, sb0 = st_.n, sv.n, ss.n
            with nc.Block() as blk:
                @blk.tensor
                def _(t):
                    nmm = 0
                    for h in range(8):
                        p, hp = h // 2, 64 * (h % 2)
                        xbk, xof = h % 2, (h // 2) * 64
                        t.matmul(pbf[:, xbk * 512 + xof: xbk * 512 + xof + 64],
                                 kA[hp:hp + 64, p * 128:(p + 1) * 128],
                                 spr[hp:hp + 64, p * 128:p * 128 + 64],
                                 start=True, stop=True).then_inc(st_.h, 1)
                        nmm += 1
                    for h in range(8):
                        ui = h * 2
                        t.wait_ge(sv.h, vb + h + 1)
                        t.matmul(pbf[:, 2 * 512 + h * 64: 2 * 512 + (h + 1) * 64],
                                 Tc[:, ui * 128:(ui + 1) * 128],
                                 xs[:, h * 64:(h + 1) * 64],
                                 start=True, stop=True).then_inc(st_.h, 1)
                        nmm += 1
                    for h in range(8):
                        p, hp = h // 2, 64 * (h % 2)
                        xbk, xof = 3 + h % 2, (h // 2) * 64
                        t.wait_ge(ss.h, sb0 + 1)
                        t.matmul(pbf[:, xbk * 512 + xof: xbk * 512 + xof + 64],
                                 kPRE[hp:hp + 64, p * 128:(p + 1) * 128],
                                 spr[hp:hp + 64, p * 128:p * 128 + 64],
                                 start=True, stop=False)
                        t.matmul(pbf[:, xbk * 512 + xof: xbk * 512 + xof + 64],
                                 gpre[:, h * 128:(h + 1) * 128],
                                 ub[:, h * 64:(h + 1) * 64],
                                 start=False, stop=True).then_inc(st_.h, 1)
                        nmm += 1
                    for h in range(8):
                        p, hp = h // 2, 64 * (h % 2)
                        xbk, xof = h % 2, (h // 2) * 64
                        t.wait_ge(sv.h, vb + 11 + 3 * h)
                        t.matmul(pbf[:, xbk * 512 + xof: xbk * 512 + xof + 64],
                                 kAR[hp:hp + 64, p * 128:(p + 1) * 128],
                                 spr[hp:hp + 64, p * 128 + 64:p * 128 + 128],
                                 start=True, stop=True).then_inc(st_.h, 1)
                        nmm += 1
                    for h in range(8):
                        ui = h * 2 + 1
                        t.wait_ge(sv.h, vb + 33 + h)
                        t.matmul(pbf[:, 2 * 512 + h * 64: 2 * 512 + (h + 1) * 64],
                                 Tc[:, ui * 128:(ui + 1) * 128],
                                 xs[:, h * 64:(h + 1) * 64],
                                 start=True, stop=True).then_inc(st_.h, 1)
                        nmm += 1
                    for h in range(8):
                        p, hp = h // 2, 64 * (h % 2)
                        t.wait_ge(ss.h, sb0 + 2)
                        t.matmul(pbf[:, 5 * 512 + h * 64: 5 * 512 + (h + 1) * 64],
                                 gq[:, h * 128:(h + 1) * 128],
                                 ub[:, h * 64:(h + 1) * 64],
                                 start=True, stop=False)
                        t.matmul(pbf[:, 5 * 512 + h * 64: 5 * 512 + (h + 1) * 64],
                                 qA[hp:hp + 64, p * 128:(p + 1) * 128],
                                 spr[hp:hp + 64, p * 128:p * 128 + 64],
                                 start=False, stop=False)
                        t.matmul(pbf[:, 5 * 512 + h * 64: 5 * 512 + (h + 1) * 64],
                                 gqR[:, h * 128:(h + 1) * 128],
                                 ubR[:, h * 64:(h + 1) * 64],
                                 start=False, stop=False)
                        t.matmul(pbf[:, 5 * 512 + h * 64: 5 * 512 + (h + 1) * 64],
                                 qAR[hp:hp + 64, p * 128:(p + 1) * 128],
                                 spr[hp:hp + 64, p * 128 + 64:p * 128 + 128],
                                 start=False, stop=True).then_inc(st_.h, 1)
                        nmm += 1
                    t.wait_ge(sv.h, vb + 32)
                    for h in range(8):
                        p, hp = h // 2, 64 * (h % 2)
                        sbk = 3 + (p % 2)
                        t.matmul(pbf[hp:hp + 64, sbk * 512 + (p // 2) * 256: sbk * 512 + (p // 2) * 256 + 64],
                                 kiat[:, p * 128 + 64 * (h % 2): p * 128 + 64 * (h % 2) + 64],
                                 ub[:, h * 64:(h + 1) * 64],
                                 start=True, stop=True).then_inc(st_.h, 1)
                        t.matmul(pbf[hp:hp + 64, sbk * 512 + (p // 2) * 256 + 64: sbk * 512 + (p // 2) * 256 + 128],
                                 kiart[:, p * 128 + 64 * (h % 2): p * 128 + 64 * (h % 2) + 64],
                                 ubR[:, h * 64:(h + 1) * 64],
                                 start=True, stop=True).then_inc(st_.h, 1)
                        nmm += 2
                    st_.n += nmm
                @blk.vector
                def _(v):
                    def vw(inst):
                        inst.then_inc(sv.h, 1); sv.n += 1; v.wait_ge(sv.h, sv.n)
                    for h in range(8):
                        xbk, xof = h % 2, (h // 2) * 64
                        v.wait_ge(st_.h, tb + 8)
                        v.scalar_tensor_tensor(xs[:, h * 64:(h + 1) * 64],
                                               pbf[:, xbk * 512 + xof: xbk * 512 + xof + 64],
                                               scl[:, NBRN2 + h:NBRN2 + h + 1],
                                               vsb[:, h * 64:(h + 1) * 64],
                                               ALU.mult, ALU.add).then_inc(sv.h, 1)
                    sv.n += 8
                    for h in range(8):
                        xbk, xof = 3 + h % 2, (h // 2) * 64
                        v.wait_ge(st_.h, tb + 24)
                        v.scalar_tensor_tensor(rtmp[:, h * 64:(h + 1) * 64],
                                               pbf[:, xbk * 512 + xof: xbk * 512 + xof + 64],
                                               scl[:, NRN + h:NRN + h + 1],
                                               vsl(h), ALU.mult, ALU.add).then_inc(sv.h, 1)
                        sv.n += 1
                        v.wait_ge(sv.h, sv.n)
                        v.tensor_scalar(rtmp[:, h * 64:(h + 1) * 64], rtmp[:, h * 64:(h + 1) * 64],
                                        -1.0, 1.0, ALU.max, ALU.min).then_inc(sv.h, 1)
                        sv.n += 1
                        v.wait_ge(sv.h, sv.n)
                        v.tensor_scalar(rs[:, h * 64:(h + 1) * 64], rtmp[:, h * 64:(h + 1) * 64],
                                        scl[:, GRN + h:GRN + h + 1], None, ALU.mult).then_inc(sv.h, 1)
                        sv.n += 1
                    for h in range(8):
                        xbk, xof = h % 2, (h // 2) * 64
                        v.wait_ge(st_.h, tb + 32)
                        v.scalar_tensor_tensor(xs[:, h * 64:(h + 1) * 64],
                                               pbf[:, xbk * 512 + xof: xbk * 512 + xof + 64],
                                               scl[:, NGRN2 + h:NGRN2 + h + 1],
                                               rs[:, h * 64:(h + 1) * 64],
                                               ALU.mult, ALU.add).then_inc(sv.h, 1)
                    sv.n += 8
                    for p in range(4):
                        sbk = 3 + (p % 2)
                        v.wait_ge(st_.h, tb + 64)
                        vw(v.tensor_tensor(spr[:, p * 128:(p + 1) * 128],
                                           pbf[:, sbk * 512 + (p // 2) * 256: sbk * 512 + (p // 2) * 256 + 128],
                                           spr[:, p * 128:(p + 1) * 128], ALU.add))
                        v.tensor_scalar(spr[:, p * 128:p * 128 + 64], spr[:, p * 128:p * 128 + 64],
                                        eclA[:, p:p + 1], None, ALU.mult)
                        vw(v.tensor_scalar(spr[:, p * 128 + 64:p * 128 + 128],
                                           spr[:, p * 128 + 64:p * 128 + 128],
                                           eclR[:, p:p + 1], None, ALU.mult))
                @blk.scalar
                def _(sc):
                    sc.wait_ge(st_.h, tb + 16)
                    sc.copy(ub[:, :], pbf[:, 2 * 512:3 * 512]).then_inc(ss.h, 1)
                    sc.wait_ge(st_.h, tb + 40)
                    sc.copy(ubR[:, :], pbf[:, 2 * 512:3 * 512]).then_inc(ss.h, 1)
                    sc.wait_ge(st_.h, tb + 48)
                    sc.copy(oT[:, :], pbf[:, 5 * 512:6 * 512]).then_inc(ss.h, 1)
                    ss.n += 3

            if not (stages2 & 8):
                continue
            tb, vb, sb0 = st_.n, sv.n, ss.n
            with nc.Block() as blk:
                @blk.tensor
                def _(t):
                    for p in range(4):
                        t.transpose(pb16[:, p * 128:(p + 1) * 128],
                                    oT[:, p * 128:(p + 1) * 128], sid[:, :]).then_inc(st_.h, 1)
                    st_.n += 4
                @blk.scalar
                def _(sc):
                    sc.wait_ge(st_.h, tb + 4)
                    for p in range(4):
                        sc.copy(ofsl[:, p * 512 + cis * 128: p * 512 + (cis + 1) * 128],
                                pb16[:, p * 128:(p + 1) * 128]).then_inc(ss.h, 1)
                    ss.n += 4
            if cis == 3:
                t0 = seg * 512
                sh, tloc = divmod(t0, T // 2)
                with nc.Block() as blk:
                    @blk.sync
                    def _(s, sh=sh, tloc=tloc):
                        for p in range(4):
                            so.inc(s.dma_start(ofd[sh * 4 + p, :, tloc:tloc + 512],
                                               ofsl[:, p * 512:(p + 1) * 512]), 16)
                        s.wait_ge(so.h, so.n)
    return nc


def build_phase3(nc, stages3=255, ext=None, pfx="", fused=False):
    TH = T // 2
    NG = TH // 512
    if ext is None:
        ext = {}
        for nm, shape, dt in [("x3", [TH, DM], F32), ("ofA", [4, 128, TH], BF16),
                              ("ofB", [4, 128, TH], BF16), ("woT", [DM, DM], F32),
                              ("abd", [TH // 128, 128, 2], F32), ("wr", [DM, 8], F32),
                              ("brv", [8], F32), ("w1", [8, DM, 2048], F32),
                              ("b1v", [8, 2048], F32), ("w2", [8, 2048, DM], F32),
                              ("b2v", [8, DM], F32), ("id16", [128, 128], BF16),
                              ("seld", [8, 8 * 128], BF16)]:
            ext[nm] = nc.dram_tensor(nm, shape, dt, kind="ExternalInput").ap()
        ext["out"] = nc.dram_tensor("out", [TH, DM], F32, kind="ExternalOutput").ap()
    x, woT, wr, brv = (ext[n] for n in ("x3", "woT", "wr", "brv"))
    w1, b1v, w2, b2v, id16, seld, out = (ext[n] for n in
        ("w1", "b1v", "w2", "b2v", "id16", "seld", "out"))
    if fused:
        ofg, abg, coreid = ext["ofg"], ext["abg"], ext["coreid"]
        ofA = ofB = abd = None
    else:
        ofA, ofB, abd = ext["ofA"], ext["ofB"], ext["abd"]
    x1d = nc.dram_tensor(pfx + "x1d", [TH, DM], F32)
    xn3d = nc.dram_tensor(pfx + "xn3d", [8, 128, TH], BF16)

    with contextlib.ExitStack() as es:
        sb = lambda n, s, d: es.enter_context(nc.sbuf_tensor(pfx + n, s, d))
        ps = lambda n, s, d: es.enter_context(nc.psum_tensor(pfx + n, s, d))
        mksem = lambda n: Ctr(es.enter_context(nc.semaphore(pfx + n)))

        sid = sb("sid", [128, 128], BF16)
        onesr = sb("onesr", [1, 512], BF16)
        wld3 = sb("wld3", [128, 2048], F32)
        wr16 = sb("wr16", [128, 64], BF16)
        brbc = sb("brbc", [128, 8], F32)
        abt = sb("abt", [128, (TH // 128) * 2], F32)
        stat3 = sb("stat3", [128, 64], F32)
        mp16 = sb("mp16", [8, TH], BF16)
        sel8 = sb("sel8", [8, 8 * 128], BF16)
        lgt = sb("lgt", [128, 8], F32)
        ecol = sb("ecol", [128, 8], F32)
        mcol = sb("mcol", [128, 8], F32)

        pbf = ps("pbf", [128, 3072], F32)
        pb16 = ps("pb16", [128, 2048], BF16)
        bank = lambda i: pbf[:, 512 * i:512 * (i + 1)]

        sd = mksem("sd")
        st_ = mksem("st_")
        sv = mksem("sv")
        ss = mksem("ss")
        so = mksem("so")
        SX3, SS3, NM3, S33, TM3, SC3 = 0, 8, 16, 24, 32, 40

        cidb = sb("cidb", [1, 1], mybir.dt.int32) if fused else None
        with nc.Block() as blk:
            @blk.sync
            def _(s):
                sd.inc(s.dma_start(sid[:, :], id16), 16)
                if not fused:
                    for tt in range(TH // 128):
                        sd.inc(s.dma_start(abt[:, tt * 2:(tt + 1) * 2], abd[tt, :, :]), 16)
                else:
                    sd.inc(s.dma_start(cidb[:, :], coreid), 16)
                for kc in range(8):
                    sd.inc(s.dma_start(wld3[:, 1024 + kc * 8:1024 + (kc + 1) * 8],
                                       wr[kc * 128:(kc + 1) * 128, :]), 16)
                sd.inc(s.dma_start(wld3[0:1, 0:8], brv), 16)
                sd.inc(s.dma_start(sel8[:, :], seld), 16)
                s.wait_ge(sd.h, sd.n)
        if fused:
            with nc.Block() as blk:
                @blk.gpsimd
                def _(g):
                    with g.register("cidr") as cidr:
                        g.reg_load(cidr, cidb[0:1, 0:1])
                        for c in range(8):
                            hh = c % 2
                            with g.If_eq(cidr, c):
                                for tt in range(TH // 128):
                                    sd.inc(g.dma_start(
                                        abt[:, tt * 2:(tt + 1) * 2],
                                        abg[c * 64 + hh * 32 + tt, :, :]), 16)
                                sd.n -= 16 * (TH // 128)
                        sd.n += 16 * (TH // 128)
                        g.wait_ge(sd.h, sd.n)
        with nc.Block() as blk:
            @blk.vector
            def _(v):
                v.memset(onesr[:, :], 1.0)
                v.memset(wld3[0:1, 8:136], 1.0)
                v.tensor_copy(wr16[:, :], wld3[:, 1024:1088])
        with nc.Block() as blk:
            @blk.tensor
            def _(t):
                t.matmul(pbf[:, 0:8], wld3[0:1, 8:136], wld3[0:1, 0:8],
                         start=True, stop=True).then_inc(st_.h, 1)
                st_.n += 1
            @blk.vector
            def _(v):
                v.wait_ge(st_.h, st_.n)
                v.tensor_copy(brbc[:, :], pbf[:, 0:8])

        if not (stages3 & 1):
            NGa = 0
        else:
            NGa = NG
        with contextlib.ExitStack() as e3:
            sba = lambda n, s, d: e3.enter_context(nc.sbuf_tensor(n, s, d))
            wo16 = sba("wo16", [128, 8 * 1024], BF16)
            ofs = sba("ofs", [128, 8 * 512], BF16)
            xtok = sba("xtok", [128, 4 * 1024], F32)
            x1t = sba("x1t", [128, 4 * 1024], F32)
            attn16 = sba("attn16", [128, 8 * 512], BF16)
            xz3 = sba("xz3", [128, 1024], BF16)
            xn3g = sba("xn3g", [128, 8 * 512], BF16)
            mprow = sba("mprow", [32, TH], F32)
            mps = sba("mps", [128, 32], F32)
            for ic in range(8):
                with nc.Block() as blk:
                    @blk.sync
                    def _(s, ic=ic):
                        sd.inc(s.dma_start(wld3[:, 0:1024], woT[ic * 128:(ic + 1) * 128, :]), 16)
                        s.wait_ge(sd.h, sd.n)
                with nc.Block() as blk:
                    @blk.vector
                    def _(v, ic=ic):
                        v.tensor_copy(wo16[:, ic * 1024:(ic + 1) * 1024], wld3[:, 0:1024])
            with nc.Block() as blk:
                @blk.vector
                def _(v):
                    v.memset(mps[:, :], 0.0)

            for g in range(NGa):
                t0 = g * 512
                with nc.Block() as blk:
                    if fused:
                        @blk.gpsimd
                        def _(g, t0=t0):
                            with g.register(f"cidr2_{t0}") as cidr:
                                g.reg_load(cidr, cidb[0:1, 0:1])
                                for c in range(8):
                                    b_, hh = c // 2, c % 2
                                    with g.If_eq(cidr, c):
                                        for ic in range(4):
                                            sd.inc(g.dma_start(
                                                ofs[:, ic * 512:(ic + 1) * 512],
                                                ofg[16 * b_ + hh * 4 + ic, :, t0:t0 + 512]), 16)
                                            sd.inc(g.dma_start(
                                                ofs[:, (4 + ic) * 512:(5 + ic) * 512],
                                                ofg[16 * b_ + 8 + hh * 4 + ic, :, t0:t0 + 512]), 16)
                                        sd.n -= 16 * 8
                                sd.n += 16 * 8
                    @blk.sync
                    def _(s, t0=t0):
                        if not fused:
                            for ic in range(4):
                                sd.inc(s.dma_start(ofs[:, ic * 512:(ic + 1) * 512],
                                                   ofA[ic, :, t0:t0 + 512]), 16)
                                sd.inc(s.dma_start(ofs[:, (4 + ic) * 512:(5 + ic) * 512],
                                                   ofB[ic, :, t0:t0 + 512]), 16)
                        for ts in range(4):
                            sd.inc(s.dma_start(xtok[:, ts * 1024:(ts + 1) * 1024],
                                               x[t0 + ts * 128: t0 + (ts + 1) * 128, :]), 16)
                        s.wait_ge(sd.h, sd.n)
                tb, vb, sb0 = st_.n, sv.n, ss.n
                with nc.Block() as blk:
                    @blk.tensor
                    def _(t):
                        for dc in range(8):
                            bk = dc % 2
                            if dc >= 2:
                                t.wait_ge(ss.h, sb0 + dc - 2 + 1)
                            mm = None
                            for ic in range(8):
                                mm = t.matmul(bank(bk)[:, 0:512],
                                              wo16[:, ic * 1024 + dc * 128: ic * 1024 + (dc + 1) * 128],
                                              ofs[:, ic * 512:(ic + 1) * 512],
                                              start=(ic == 0), stop=(ic == 7))
                            mm.then_inc(st_.h, 1)
                        st_.n += 8
                    @blk.scalar
                    def _(sc):
                        for dc in range(8):
                            bk = dc % 2
                            sc.wait_ge(st_.h, tb + dc + 1)
                            sc.copy(attn16[:, dc * 512:(dc + 1) * 512],
                                    bank(bk)[:, 0:512]).then_inc(ss.h, 1)
                        ss.n += 8
                tb, vb, sb0 = st_.n, sv.n, ss.n
                with nc.Block() as blk:
                    @blk.tensor
                    def _(t):
                        for ts in range(4):
                            if ts >= 2:
                                t.wait_ge(sv.h, vb + 2 * (ts - 2) + 2)
                            for dc in range(8):
                                t.transpose(pb16[:, (ts % 2) * 1024 + dc * 128:
                                            (ts % 2) * 1024 + (dc + 1) * 128],
                                            attn16[:, dc * 512 + ts * 128: dc * 512 + (ts + 1) * 128],
                                            sid[:, :]).then_inc(st_.h, 1)
                        st_.n += 32
                    @blk.vector
                    def _(v):
                        for ts in range(4):
                            tt = g * 4 + ts
                            v.tensor_scalar(x1t[:, ts * 1024:(ts + 1) * 1024],
                                            xtok[:, ts * 1024:(ts + 1) * 1024],
                                            abt[:, tt * 2:tt * 2 + 1],
                                            abt[:, tt * 2 + 1:tt * 2 + 2],
                                            ALU.mult, ALU.add).then_inc(sv.h, 1)
                            sv.n += 1
                            v.wait_ge(sv.h, sv.n)
                            v.wait_ge(st_.h, tb + 8 * (ts + 1))
                            v.tensor_tensor(x1t[:, ts * 1024:(ts + 1) * 1024],
                                            pb16[:, (ts % 2) * 1024:(ts % 2) * 1024 + 1024],
                                            x1t[:, ts * 1024:(ts + 1) * 1024],
                                            ALU.add).then_inc(sv.h, 1)
                            sv.n += 1
                with nc.Block() as blk:
                    @blk.scalar
                    def _(sc):
                        for ts in range(4):
                            sc.activation(wld3[:, 0:1024], x1t[:, ts * 1024:(ts + 1) * 1024],
                                          AF.Square, accum_out=stat3[:, SS3 + ts:SS3 + ts + 1])
                    @blk.vector
                    def _(v):
                        for ts in range(4):
                            v.reduce_sum(stat3[:, SX3 + ts:SX3 + ts + 1],
                                         x1t[:, ts * 1024:(ts + 1) * 1024], axis=AXX)
                A3 = lambda c: stat3[:, c:c + 4]
                with nc.Block() as blk:
                    @blk.vector
                    def _(v):
                        def vw(inst):
                            inst.then_inc(sv.h, 1); sv.n += 1; v.wait_ge(sv.h, sv.n)
                        v.tensor_scalar(A3(NM3), A3(SX3), -1.0 / DM, None, ALU.mult)
                        vw(v.tensor_scalar(A3(SC3), A3(SS3), 1.0 / DM, None, ALU.mult))
                        vw(v.tensor_tensor(A3(TM3), A3(NM3), A3(NM3), ALU.mult))
                        vw(v.tensor_tensor(A3(SC3), A3(SC3), A3(TM3), ALU.subtract))
                        vw(v.tensor_scalar(A3(TM3), A3(SC3), EPS, None, ALU.add))
                with nc.Block() as blk:
                    @blk.scalar
                    def _(sc):
                        sc.activation(A3(S33), A3(TM3), AF.Ln).then_inc(ss.h, 1)
                        ss.n += 1
                        sc.wait_ge(ss.h, ss.n)
                        sc.activation(A3(S33), A3(S33), AF.Exp, scale=-0.5).then_inc(ss.h, 1)
                        ss.n += 1
                with nc.Block() as blk:
                    @blk.vector
                    def _(v):
                        def vw(inst):
                            inst.then_inc(sv.h, 1); sv.n += 1; v.wait_ge(sv.h, sv.n)
                        vw(v.tensor_tensor(A3(SC3), A3(S33), A3(S33), ALU.mult))
                        vw(v.tensor_tensor(A3(SC3), A3(SC3), A3(TM3), ALU.mult))
                        vw(v.tensor_scalar(A3(SC3), A3(SC3), -0.5, 1.5, ALU.mult, ALU.add))
                        vw(v.tensor_tensor(A3(S33), A3(S33), A3(SC3), ALU.mult))
                tb, vb, sb0 = st_.n, sv.n, ss.n
                with nc.Block() as blk:
                    @blk.vector
                    def _(v):
                        for ts in range(4):
                            if ts > 0:
                                v.wait_ge(st_.h, tb + 8 * ts)
                            v.tensor_scalar(xz3[:, :], x1t[:, ts * 1024:(ts + 1) * 1024],
                                            stat3[:, NM3 + ts:NM3 + ts + 1],
                                            stat3[:, S33 + ts:S33 + ts + 1],
                                            ALU.add, ALU.mult).then_inc(sv.h, 1)
                            sv.n += 1
                    @blk.tensor
                    def _(t):
                        for ts in range(4):
                            t.wait_ge(sv.h, vb + ts + 1)
                            if ts >= 2:
                                t.wait_ge(ss.h, sb0 + 8 * (ts - 1))
                            for dc in range(8):
                                t.transpose(pb16[:, (ts % 2) * 1024 + dc * 128:
                                            (ts % 2) * 1024 + (dc + 1) * 128],
                                            xz3[:, dc * 128:(dc + 1) * 128],
                                            sid[:, :]).then_inc(st_.h, 1)
                        st_.n += 32
                    @blk.scalar
                    def _(sc):
                        for ts in range(4):
                            sc.wait_ge(st_.h, tb + 8 * (ts + 1))
                            for dc in range(8):
                                sc.copy(xn3g[:, dc * 512 + ts * 128: dc * 512 + (ts + 1) * 128],
                                        pb16[:, (ts % 2) * 1024 + dc * 128:
                                             (ts % 2) * 1024 + (dc + 1) * 128]).then_inc(ss.h, 1)
                        ss.n += 32
                tb, vb, sb0 = st_.n, sv.n, ss.n
                with nc.Block() as blk:
                    @blk.tensor
                    def _(t):
                        for ts in range(4):
                            if ts > 0:
                                t.wait_ge(sv.h, vb + 8 * (ts - 1) + 1)
                            mm = None
                            for kc in range(8):
                                mm = t.matmul(pbf[:, 2 * 512:2 * 512 + 8],
                                              xn3g[:, kc * 512 + ts * 128: kc * 512 + (ts + 1) * 128],
                                              wr16[:, kc * 8:(kc + 1) * 8],
                                              start=(kc == 0), stop=(kc == 7))
                            mm.then_inc(st_.h, 1)
                        st_.n += 4
                    @blk.vector
                    def _(v, t0=t0):
                        def vw(inst):
                            inst.then_inc(sv.h, 1); sv.n += 1; v.wait_ge(sv.h, sv.n)
                        for ts in range(4):
                            v.wait_ge(st_.h, tb + ts + 1)
                            vw(v.tensor_tensor(lgt[:, :], pbf[:, 2 * 512:2 * 512 + 8],
                                               brbc[:, :], ALU.add))
                            vw(v.reduce_max(mcol[:, 0:1], lgt[:, :], axis=AXX))
                            vw(v.tensor_scalar(mcol[:, 1:2], mcol[:, 0:1], -1.0, None, ALU.mult))
                            v.wait_ge(ss.h, sb0 + ts + 1)
                            vw(v.reduce_sum(mcol[:, 2:3], ecol[:, :], axis=AXX))
                            vw(v.reciprocal(mcol[:, 3:4], mcol[:, 2:3]))
                            vw(v.tensor_scalar(ecol[:, :], lgt[:, :], mcol[:, 0:1], None,
                                               ALU.is_ge))
                            vw(v.tensor_scalar(mps[:, 0:8], ecol[:, :], mcol[:, 3:4], None,
                                               ALU.mult))
                            lastt = None
                            for bi in range(4):
                                lastt = v.transpose(
                                    mprow[0:32, t0 + ts * 128 + 32 * bi: t0 + ts * 128 + 32 * (bi + 1)],
                                    mps[32 * bi:32 * (bi + 1), 0:32])
                            vw(lastt)
                    @blk.scalar
                    def _(sc):
                        for ts in range(4):
                            sc.wait_ge(sv.h, vb + 8 * ts + 3)
                            sc.activation(ecol[:, :], lgt[:, :], AF.Exp,
                                          bias=mcol[:, 1:2]).then_inc(ss.h, 1)
                        ss.n += 4
                with nc.Block() as blk:
                    @blk.sync
                    def _(s, t0=t0):
                        for ts in range(4):
                            so.inc(s.dma_start(x1d[t0 + ts * 128: t0 + (ts + 1) * 128, :],
                                               x1t[:, ts * 1024:(ts + 1) * 1024]), 16)
                        for dc in range(8):
                            so.inc(s.dma_start(xn3d[dc, :, t0:t0 + 512],
                                               xn3g[:, dc * 512:(dc + 1) * 512]), 16)
                        s.wait_ge(so.h, so.n)
            with nc.Block() as blk:
                @blk.vector
                def _(v):
                    v.tensor_copy(mp16[:, :], mprow[0:8, :])

        with contextlib.ExitStack() as eacc:
            acc = eacc.enter_context(nc.sbuf_tensor(pfx + "acc", [128, 8 * TH], BF16))
            with nc.Block() as blk:
                @blk.vector
                def _(v):
                    v.memset(acc[:, :], 0.0)
            with contextlib.ExitStack() as e3:
                sba = lambda n, s, d: e3.enter_context(nc.sbuf_tensor(n, s, d))
                w116 = sba("w116", [128, 8 * 2048], BF16)
                w216 = sba("w216", [128, 16 * 1024], BF16)
                hm16 = sba("hm16", [128, 2 * 16 * 512], BF16)
                xn3s = sba("xn3s", [128, 2 * 8 * 512], BF16)
                mbc = sba("mbc", [128, 1024], BF16)
                b1r = sba("b1r", [1, 2048], BF16)
                b2t = sba("b2t", [8, 1024], BF16)
                sdw = Ctr(e3.enter_context(nc.semaphore(pfx + "sdw")))
                w1base = {}
                w2base = {}
                xnbase = {}

                def ld_w1(g, e):
                    w1base[e] = sdw.n
                    for kc in range(8):
                        sdw.inc(g.dma_start(w116[:, kc * 2048:(kc + 1) * 2048],
                                            w1[e, kc * 128:(kc + 1) * 128, :]), 16)
                    sdw.inc(g.dma_start(b1r[:, :], b1v[e, :]), 16)

                def ld_w2(g, e):
                    w2base[e] = sdw.n
                    for kc in range(16):
                        sdw.inc(g.dma_start(w216[:, kc * 1024:(kc + 1) * 1024],
                                            w2[e, kc * 128:(kc + 1) * 128, :]), 16)

                def ld_xn3(s, e, jp):
                    xnbase[(e, jp)] = sd.n
                    for gpar in range(2):
                        g_ = 2 * jp + gpar
                        for dc in range(8):
                            sd.inc(s.dma_start(
                                xn3s[:, gpar * 4096 + dc * 512: gpar * 4096 + (dc + 1) * 512],
                                xn3d[dc, :, g_ * 512:(g_ + 1) * 512]), 16)

                if stages3 & 2:
                    with nc.Block() as blk:
                        @blk.gpsimd
                        def _(g):
                            ld_w1(g, 0)
                            ld_w2(g, 0)
                            sdw.inc(g.dma_start(b2t[:, :], b2v), 16)
                        @blk.sync
                        def _(s):
                            ld_xn3(s, 0, 0)
                for e in range(8 if (stages3 & 2) else 0):
                    for jp in range(4):
                        t0A, t0B = (2 * jp) * 512, (2 * jp + 1) * 512
                        tb, vb, sb0 = st_.n, sv.n, ss.n
                        with nc.Block() as blk:
                            @blk.gpsimd
                            def _(g, e=e, jp=jp, tb=tb):
                                if jp == 0 and e > 0:
                                    ld_w2(g, e)
                                g.wait_ge(st_.h, tb + 18)
                                if jp == 3:
                                    if e < 7:
                                        ld_w1(g, e + 1)
                            @blk.sync
                            def _(s, e=e, jp=jp, tb=tb):
                                if jp < 3 or e < 7:
                                    s.wait_ge(st_.h, tb + 18)
                                    if jp < 3:
                                        ld_xn3(s, e, jp + 1)
                                    else:
                                        ld_xn3(s, e + 1, 0)
                            @blk.tensor
                            def _(t, e=e, jp=jp, tb=tb, vb=vb, t0A=t0A, t0B=t0B):
                                xb = xnbase[(e, jp)]
                                t.matmul(bank(4)[:, 0:512], sel8[:, e * 128:(e + 1) * 128],
                                         mp16[:, t0A:t0A + 512],
                                         start=True, stop=True).then_inc(st_.h, 1)
                                t.matmul(bank(5)[:, 0:512], sel8[:, e * 128:(e + 1) * 128],
                                         mp16[:, t0B:t0B + 512],
                                         start=True, stop=True).then_inc(st_.h, 1)
                                t.wait_ge(sd.h, xb + 256)
                                for hc in range(16):
                                    bA = (hc % 2) * 2
                                    if hc >= 2:
                                        t.wait_ge(sv.h, vb + 2 * (hc - 1))
                                    for kc in range(8):
                                        if jp == 0:
                                            t.wait_ge(sdw.h, w1base[e] + 16 * (kc + 1))
                                        for gp, bk in ((0, bA), (1, bA + 1)):
                                            t.matmul(bank(bk)[:, 0:512],
                                                     w116[:, kc * 2048 + hc * 128: kc * 2048 + (hc + 1) * 128],
                                                     xn3s[:, gp * 4096 + kc * 512: gp * 4096 + (kc + 1) * 512],
                                                     start=(kc == 0), stop=False)
                                    if jp == 0:
                                        t.wait_ge(sdw.h, w1base[e] + 16 * 9)
                                    for gp, bk in ((0, bA), (1, bA + 1)):
                                        mm = t.matmul(bank(bk)[:, 0:512],
                                                      b1r[0:1, hc * 128:(hc + 1) * 128],
                                                      onesr[0:1, 0:512],
                                                      start=False, stop=True)
                                    mm.then_inc(st_.h, 1)
                                for dc in range(8):
                                    bA = (dc % 2) * 2
                                    t.wait_ge(sv.h, vb + 32 + (0 if dc < 2 else 2 * (dc - 1)))
                                    for kc in range(16):
                                        if jp == 0:
                                            t.wait_ge(sdw.h, w2base[e] + 16 * (kc + 1))
                                        for gp, bk in ((0, bA), (1, bA + 1)):
                                            mm = t.matmul(bank(bk)[:, 0:512],
                                                          w216[:, kc * 1024 + dc * 128: kc * 1024 + (dc + 1) * 128],
                                                          hm16[:, gp * 8192 + kc * 512: gp * 8192 + (kc + 1) * 512],
                                                          start=(kc == 0),
                                                          stop=(kc == 15 and e != 7))
                                    if e == 7:
                                        for gp, bk, t0g in ((0, bA, t0A), (1, bA + 1, t0B)):
                                            mm = t.matmul(bank(bk)[:, 0:512],
                                                          b2t[:, dc * 128:(dc + 1) * 128],
                                                          mp16[:, t0g:t0g + 512],
                                                          start=False, stop=True)
                                    mm.then_inc(st_.h, 1)
                                st_.n += 26
                            @blk.scalar
                            def _(sc, tb=tb):
                                for gp in range(2):
                                    sc.wait_ge(st_.h, tb + 1 + gp)
                                    sc.copy(mbc[:, gp * 512:(gp + 1) * 512],
                                            bank(4 + gp)[:, 0:512]).then_inc(ss.h, 1)
                                ss.n += 2
                            @blk.vector
                            def _(v, tb=tb, sb0=sb0, t0A=t0A):
                                for hc in range(16):
                                    bA = (hc % 2) * 2
                                    v.wait_ge(st_.h, tb + 2 + hc + 1)
                                    for gp in range(2):
                                        v.wait_ge(ss.h, sb0 + 1 + gp)
                                        v.scalar_tensor_tensor(
                                            hm16[:, gp * 8192 + hc * 512: gp * 8192 + (hc + 1) * 512],
                                            bank(bA + gp)[:, 0:512], 0.0,
                                            mbc[:, gp * 512:(gp + 1) * 512],
                                            ALU.max, ALU.mult).then_inc(sv.h, 1)
                                        sv.n += 1
                                for dc in range(8):
                                    bA = (dc % 2) * 2
                                    v.wait_ge(st_.h, tb + 18 + dc + 1)
                                    for gp in range(2):
                                        t0g = t0A + gp * 512
                                        v.tensor_tensor(acc[:, dc * TH + t0g: dc * TH + t0g + 512],
                                                        bank(bA + gp)[:, 0:512],
                                                        acc[:, dc * TH + t0g: dc * TH + t0g + 512],
                                                        ALU.add).then_inc(sv.h, 1)
                                        sv.n += 1
            with contextlib.ExitStack() as e3:
                sba = lambda n, s, d: e3.enter_context(nc.sbuf_tensor(n, s, d))
                x1b = sba("x1b", [128, 4 * 1024], F32)
                outb = sba("outb", [128, 4 * 1024], F32)
                for g in range(NG if (stages3 & 4) else 0):
                    t0 = g * 512
                    with nc.Block() as blk:
                        @blk.sync
                        def _(s, t0=t0):
                            for ts in range(4):
                                sd.inc(s.dma_start(x1b[:, ts * 1024:(ts + 1) * 1024],
                                                   x1d[t0 + ts * 128: t0 + (ts + 1) * 128, :]), 16)
                            s.wait_ge(sd.h, sd.n)
                    tb, vb, sb0 = st_.n, sv.n, ss.n
                    with nc.Block() as blk:
                        @blk.tensor
                        def _(t, t0=t0):
                            for ts in range(4):
                                if ts >= 2:
                                    t.wait_ge(sv.h, vb + ts - 2 + 1)
                                for dc in range(8):
                                    t.transpose(pb16[:, (ts % 2) * 1024 + dc * 128:
                                                (ts % 2) * 1024 + (dc + 1) * 128],
                                                acc[:, dc * TH + t0 + ts * 128: dc * TH + t0 + (ts + 1) * 128],
                                                sid[:, :]).then_inc(st_.h, 1)
                            st_.n += 32
                        @blk.vector
                        def _(v):
                            for ts in range(4):
                                v.wait_ge(st_.h, tb + 8 * (ts + 1))
                                v.tensor_tensor(outb[:, ts * 1024:(ts + 1) * 1024],
                                                pb16[:, (ts % 2) * 1024:(ts % 2) * 1024 + 1024],
                                                x1b[:, ts * 1024:(ts + 1) * 1024],
                                                ALU.add).then_inc(sv.h, 1)
                                sv.n += 1
                    with nc.Block() as blk:
                        @blk.sync
                        def _(s, t0=t0):
                            for ts in range(4):
                                so.inc(s.dma_start(out[t0 + ts * 128: t0 + (ts + 1) * 128, :],
                                                   outb[:, ts * 1024:(ts + 1) * 1024]), 16)
                            s.wait_ge(so.h, so.n)
    return nc


def _hostprep_phase3(inputs, core, ofds):
    b, hh = core // 2, core % 2
    f = np.float32
    cc = np.ascontiguousarray
    TH = T // 2
    t0 = hh * TH
    return {
        "x": cc(np.asarray(inputs["L0_attn_x"], f)[b, t0:t0 + TH]),
        "ofA": cc(ofds[2 * b][:, :, t0:t0 + TH]),
        "ofB": cc(ofds[2 * b + 1][:, :, t0:t0 + TH]),
        "woT": cc(np.asarray(inputs["L0_attn_Wo"], f).T),
        "wr": cc(np.asarray(inputs["L0_ffn_Wr"], f)),
        "brv": cc(np.asarray(inputs["L0_ffn_br"], f)),
        "w1": cc(np.asarray(inputs["L0_ffn_W1"], f)),
        "b1v": cc(np.asarray(inputs["L0_ffn_b1"], f)),
        "w2": cc(np.asarray(inputs["L0_ffn_W2"], f)),
        "b2v": cc(np.asarray(inputs["L0_ffn_b2"], f)),
        "id16": np.eye(128, dtype=BF).view(np.uint16),
    }


def _consts2():
    mks = np.concatenate([np.tril(np.ones((128, 128), BF), -1),
                          np.triu(np.ones((128, 128), BF), 1),
                          np.triu(np.ones((128, 128), BF), 0)], axis=1)
    return mks.view(np.uint16)


def _sel8():
    s = np.zeros((8, 8 * 128), dtype=BF)
    for e in range(8):
        s[e, e * 128:(e + 1) * 128] = 1
    return s.view(np.uint16)


def build_fused(nc):
    TH = T // 2
    NTl = T // 128
    ext = {}
    ins = [("x", [T, DM], F32), ("wqT", [DM, 512], F32), ("wkT", [DM, 512], F32),
           ("waT", [DM, 512], F32), ("waRT", [DM, 512], F32), ("wvbgT", [DM, 528], F32),
           ("bav", [512], F32), ("baRv", [512], F32), ("bbgv", [16], F32),
           ("lnwv", [DM], F32), ("lnbv", [DM], F32), ("id16", [128, 128], BF16),
           ("mks", [128, 3 * 128], BF16), ("x3", [TH, DM], F32), ("woT", [DM, DM], F32),
           ("wr", [DM, 8], F32), ("brv", [8], F32), ("w1", [8, DM, 2048], F32),
           ("b1v", [8, 2048], F32), ("w2", [8, 2048, DM], F32), ("b2v", [8, DM], F32),
           ("seld", [8, 8 * 128], BF16)]
    for nm, shape, dt in ins:
        ext[nm] = nc.dram_tensor(nm, shape, dt, kind="ExternalInput").ap()
    ext["coreid"] = nc.dram_tensor("coreid", [1, 1], mybir.dt.int32,
                                   kind="ExternalInput").ap()
    ext["out"] = nc.dram_tensor("out", [TH, DM], F32, kind="ExternalOutput").ap()
    intr = [("qd", [4, 128, T], BF16), ("kd", [4, 128, T], BF16),
            ("lad", [4, 128, T], F32), ("laRd", [4, 128, T], F32),
            ("vtd", [NTl, 128, 512], BF16), ("tgd", [NTl, 128, 26], F32)]
    for nm, shape, dt in intr:
        ext[nm] = nc.dram_tensor(nm, shape, dt).ap()
    abx_t = nc.dram_tensor("abx", [NTl, 128, 2], F32)
    ofd_t = nc.dram_tensor("ofd", [8, 128, TH], BF16)
    ofg_t = nc.dram_tensor("ofg", [64, 128, TH], BF16)
    abg_t = nc.dram_tensor("abg", [8 * NTl, 128, 2], F32)
    ext["abx"] = abx_t.ap()
    ext["ofd"] = ofd_t.ap()
    ext["ofg"] = ofg_t.ap()
    ext["abg"] = abg_t.ap()
    build_phase1(nc, ext=ext, pfx="p1_")
    build_phase2(nc, ext=ext, pfx="p2_")
    with nc.semaphore("ccsem") as ccsem:
        with nc.Block() as blk:
            @blk.gpsimd
            def _(g):
                g.collective_compute(
                    "AllGather", mybir.AluOpType.bypass,
                    replica_groups=[list(range(8))],
                    ins=[ofd_t.ap().opt()], outs=[ofg_t.ap().opt()],
                ).then_inc(ccsem, 1)
                g.collective_compute(
                    "AllGather", mybir.AluOpType.bypass,
                    replica_groups=[list(range(8))],
                    ins=[abx_t.ap().opt()], outs=[abg_t.ap().opt()],
                ).then_inc(ccsem, 1)
                g.wait_ge(ccsem, 2)
        build_phase3(nc, ext=ext, pfx="p3_", fused=True)
    return nc


def _hostprep_fused(inputs, core):
    m = _hostprep_phase1(inputs, core)
    b, hh = core // 2, core % 2
    TH = T // 2
    f = np.float32
    cc = np.ascontiguousarray
    m["mks"] = _consts2()
    m["x3"] = cc(np.asarray(inputs["L0_attn_x"], f)[b, hh * TH:(hh + 1) * TH])
    m["woT"] = cc(np.asarray(inputs["L0_attn_Wo"], f).T)
    m["wr"] = cc(np.asarray(inputs["L0_ffn_Wr"], f))
    m["brv"] = cc(np.asarray(inputs["L0_ffn_br"], f))
    m["w1"] = cc(np.asarray(inputs["L0_ffn_W1"], f))
    m["b1v"] = cc(np.asarray(inputs["L0_ffn_b1"], f))
    m["w2"] = cc(np.asarray(inputs["L0_ffn_W2"], f))
    m["b2v"] = cc(np.asarray(inputs["L0_ffn_b2"], f))
    m["seld"] = _sel8()
    m["coreid"] = np.array([[core]], dtype=np.int32)
    return m


def kernel(**inputs):
    nc = bass.Bass("TRN2", target_bir_lowering=False)
    build_fused(nc)
    im = [_hostprep_fused(inputs, c) for c in range(8)]
    res = run_bass_kernel_spmd(nc, im, core_ids=list(range(8)))
    out = np.zeros((B, T, DM), np.float32)
    TH = T // 2
    for c in range(8):
        b, hh = c // 2, c % 2
        out[b, hh * TH:(hh + 1) * TH] = np.asarray(res.results[c]["out"])
    return out
